# version 16
# baseline (speedup 1.0000x reference)
import numpy as np
import ml_dtypes
import concourse.bass as bass
import concourse.mybir as mybir
from concourse.bass_utils import run_bass_kernel_spmd

F32 = mybir.dt.float32
BF16 = mybir.dt.bfloat16
I32 = mybir.dt.int32
AF = mybir.ActivationFunctionType
ALU = mybir.AluOpType
AX = mybir.AxisListType

MAGIC = 12582912.0


class Cfg:
    def __init__(self, T=16384, D=1024, detect_races=False):
        self.detect_races = detect_races
        self.B = 2
        self.T = T
        self.D = D
        self.QT = 256
        self.KG = 1024
        self.NC = D // 128
        self.NH = D // 512
        self.NTILES = T // self.QT
        self.NM = self.NTILES // 4
        self.KSN = self.KG // 128
        self.PTOK = self.NM * 512
        self.OTOK = self.NM * 256
        assert self.NM * self.KG == T


def schedule(cfg):
    out = []
    for c in range(8):
        i = (c >> 1) & 1
        r = c & 1
        per_m = []
        for m in range(cfg.NM):
            t4 = [4 * m + j for j in range(4)]
            if i == 0:
                per_m.append((t4[r], t4[2 + r]))
            else:
                per_m.append((t4[2 + r], t4[r]))
        out.append(per_m)
    return out


def host_masks(cfg, sched, c):
    per_m = sched[c]
    base = (c // 2) * 2
    quads = []
    for m in range(cfg.NM):
        a0, s0 = sched[base][m]
        a1, s1 = sched[base + 1][m]
        quads.append([a0, s0, a1, s1])
    M = np.zeros((cfg.NM, cfg.KSN, 128, 512), dtype=np.float32)
    for m, (t_av, t_so) in enumerate(per_m):
        for ks in range(cfg.KSN):
            sec_tile = quads[m][ks // 2]
            k = sec_tile * cfg.QT + (ks % 2) * 128 + np.arange(128)[:, None]
            qa = t_av * cfg.QT + np.arange(256)[None, :]
            qs = t_so * cfg.QT + np.arange(256)[None, :]
            M[m, ks, :, 0:256] = (qa >= k)
            M[m, ks, :, 256:512] = (qs >= k)
    return M.astype(ml_dtypes.bfloat16)


class Builder:
    def __init__(self, cfg):
        self.cfg = cfg
        self.dry = True
        self.marks = {}
        self.cnt = {}

    def inc(self, ins, sem, n=1, key=None):
        self.cnt[sem] = self.cnt.get(sem, 0) + n
        if key is not None:
            self.marks[(sem,) + tuple(key)] = self.cnt[sem]
        if not self.dry and ins is not None:
            ins.then_inc(self.S[sem], n)

    def mark(self, sem, *key):
        self.marks[(sem,) + tuple(key)] = self.cnt.get(sem, 0)

    def w(self, e, sem, *key):
        if not self.dry:
            e.wait_ge(self.S[sem], self.marks[(sem,) + tuple(key)])

    def ch(self, e, sem):
        if not self.dry and self.cnt.get(sem, 0) > 0:
            e.wait_ge(self.S[sem], self.cnt[sem])

    def wv(self, e, sem, val):
        if not self.dry and val > 0:
            e.wait_ge(self.S[sem], val)

    def op(self, fn, sem=None, n=1, key=None):
        ins = None
        if not self.dry:
            ins = fn()
        if sem is not None:
            self.inc(ins, sem, n, key)
        elif key is not None:
            raise ValueError("key without sem")

    def build(self):
        cfg = self.cfg
        T, D, NC, NM, KSN = cfg.T, cfg.D, cfg.NC, cfg.NM, cfg.KSN
        nc = self.nc = bass.Bass("TRN2", target_bir_lowering=False,
                                 detect_race_conditions=self.cfg.detect_races)

        g = self.g = {}
        g['x_in'] = nc.dram_tensor("x_in", [cfg.PTOK, D], F32, kind="ExternalInput").ap()
        for nm in ("wq_in", "wk_in", "wv_in", "fw_in"):
            g[nm] = nc.dram_tensor(nm, [D, D], F32, kind="ExternalInput").ap()
        for nm in ("bq_in", "bk_in", "bv_in", "fb_in"):
            g[nm] = nc.dram_tensor(nm, [1, D], F32, kind="ExternalInput").ap()
        g['msk_in'] = nc.dram_tensor("msk_in", [KSN, 128, 512], BF16,
                                     kind="ExternalInput").ap()
        g['id_in'] = nc.dram_tensor("id_in", [128, 128], BF16,
                                    kind="ExternalInput").ap()
        g['tbl_in'] = nc.dram_tensor("tbl_in", [1, max(2 * NM, 16)], I32,
                                     kind="ExternalInput").ap()
        g['out_t'] = nc.dram_tensor("out_t", [cfg.OTOK, D], F32,
                                    kind="ExternalOutput").ap()

        g['scr_wq'] = nc.dram_tensor("scr_wq", [D, D], BF16)
        g['scr_q'] = nc.dram_tensor("scr_q", [NM, 128, NC, 512], BF16)
        g['scr_base'] = nc.dram_tensor("scr_base", [cfg.OTOK, D], F32)
        g['scr_vec'] = nc.dram_tensor("scr_vec", [1, D], F32)

        g['sh_k'] = nc.dram_tensor("sh_k", [2, 128, NM, NC, 512], BF16, addr_space="Shared")
        g['sh_v'] = nc.dram_tensor("sh_v", [2, T // 2, D], BF16, addr_space="Shared")
        g['sh_z'] = nc.dram_tensor("sh_z", [2, 128], F32, addr_space="Shared")
        g['sh_fl'] = nc.dram_tensor("sh_fl", [4, 128], I32, addr_space="Shared")

        seqlist = [("tr", 0, b) for b in range(NC)]
        for m_ in range(NM):
            if m_ + 1 < NM:
                seqlist += [("tr", m_ + 1, b) for b in range(NC)]
            seqlist += [("tq", m_, b) for b in range(4)]
        self.ttinfo = {}
        for i_, u_ in enumerate(seqlist):
            self.ttinfo[u_] = (i_ % 2, seqlist[i_ - 2] if i_ >= 2 else None)

        self.jobs = [(gg, m) for gg in range(NM) for m in range(gg, NM)]
        self.NJOB = len(self.jobs)
        self.ZCOLS = self.NJOB * KSN
        self.last_job_of_g = {}
        for i, (gg, m) in enumerate(self.jobs):
            self.last_job_of_g[gg] = i

        from contextlib import ExitStack
        with ExitStack() as ctx:
            E = ctx.enter_context
            sb = self.sb = {}

            def S_(name, shape, dt, EE=None):
                sb[name] = (EE or E)(nc.sbuf_tensor(name, shape, dt))

            S_("bq", [128, NC], F32)
            S_("bk", [128, NC], F32)
            S_("wsc", [128, NC, 4], F32)
            S_("ones", [128, 2], F32)
            S_("onesrow", [1, 128], F32)
            S_("tbl", [1, max(2 * NM, 16)], I32)
            S_("zacc", [128, max(self.ZCOLS, 4)], F32)
            S_("flag", [1, 128], I32)
            S_("poll", [1, 128], I32)
            S_("zme", [1, 128], F32)
            S_("zoth", [1, 128], F32)
            S_("alph", [128, 4], F32)
            S_("cst", [128, 4], F32)
            S_("idm", [128, 128], BF16)
            p1_stack = ExitStack()
            E1 = p1_stack.enter_context
            S_("bvr", [128, D], F32, E1)
            S_("fbr", [128, D], F32, E1)
            S_("wsr", [128, D], F32, E1)
            S_("wq", [128, NC, D], BF16, E1)
            S_("wk", [128, NC, D], BF16, E1)
            S_("wv", [128, NC, D], BF16, E1)
            S_("fwt", [128, NC, D], BF16, E1)
            S_("rows", [1, D], F32, E1)
            S_("x", [128, 2, 4, D], F32, E1)
            S_("xnf", [128, 2, NC, 512], BF16, E1)
            S_("xqf", [128, 2, NC, 256], BF16, E1)
            S_("xnt", [128, 4, D], BF16, E1)
            S_("xqt", [128, 2, D], BF16, E1)
            S_("scr", [128, 2, D], F32, E1)
            S_("t0", [128, 2, D], F32, E1)
            S_("st", [128, 2, 4, 12], F32, E1)
            S_("amx", [128, 2, 2, 4], F32, E1)
            S_("qst", [128, NC, 512], BF16, E1)
            S_("kst", [128, NC, 512], BF16, E1)
            p0_stack = ExitStack()
            S_("wst", [128, 2, D], F32, p0_stack.enter_context)
            p0_stack.close()
            S_("vst", [128, 4, D], BF16, E1)
            S_("bst", [128, 2, D], F32, E1)
            p1_stack.close()
            S_("usb", [128, NM, 2, D], BF16)
            S_("kb", [128, 2, 2, NC, 512], BF16)
            S_("vb", [128, 2, KSN, D], BF16)
            S_("qb", [128, 2, NC, 512], BF16)
            S_("et", [128, 2, KSN, 512], BF16)
            S_("mkb", [128, KSN, 512], BF16)
            S_("ob", [128, 2, D], F32)
            S_("bb", [128, 2, D], F32)

            ps = self.ps = {}
            ps['a0'] = E(nc.psum_tensor("ps_a0", [128, 512], F32))
            ps['a1'] = E(nc.psum_tensor("ps_a1", [128, 512], F32))
            ps['b0'] = E(nc.psum_tensor("ps_b0", [128, D], F32))
            ps['b1'] = E(nc.psum_tensor("ps_b1", [128, D], F32))
            ps['pt'] = E(nc.psum_tensor("ps_pt", [128, 2 * D], BF16))

            sems = ["sx0", "sx1", "sw", "sww0", "sww1", "sfq",
                    "sfwt", "pe1", "act1",
                    "dve1", "pool1", "shk", "shv", "gp0", "gq2", "gb1", "bar",
                    "skv0", "skv1", "svb0", "svb1", "sqj0", "sqj1", "smk0", "smk1", "pesc",
                    "aexp", "dmask", "peav", "dveu", "zbar", "sbase0",
                    "sbase1", "sout0", "sout1", "pe0", "dve0", "casts", "grow"]
            self.S = {n: E(nc.semaphore(n)) for n in sems}
            block = E(nc.Block())

            self.dry = True
            self.cnt = {}
            for f in (self.em_sync, self.em_gpsimd, self.em_scalar, self.em_vector,
                      self.em_tensor):
                f(None)
            dry_cnt = dict(self.cnt)
            self.dry = False
            self.cnt = {}
            block.sync(self.em_sync)
            block.gpsimd(self.em_gpsimd)
            block.scalar(self.em_scalar)
            block.vector(self.em_vector)
            block.tensor(self.em_tensor)
            assert dry_cnt == self.cnt, (dry_cnt, self.cnt)
        return nc

    def em_sync(self, e):
        cfg, sb, g = self.cfg, self.sb, self.g
        NM, NC, KSN, KG = cfg.NM, cfg.NC, cfg.KSN, cfg.KG
        dry = self.dry

        def dma(out, in_, sem, key=None, **kw):
            ins = None if dry else e.dma_start(out=out, in_=in_, **kw)
            self.inc(ins, sem, 16, key)

        dma(sb['tbl'][:, :], g['tbl_in'], "sw", key=("tbl",))
        dma(sb['idm'][:, :], g['id_in'], "sw", key=("idm",))
        pr = None
        if not dry:
            pid = e.partition_id()
            pr = e.to_reg(pid % 2)

        srcs = (g['fw_in'], g['wq_in'], g['wk_in'], g['wv_in'])
        for gidx in range(4 * NC):
            src = srcs[gidx // NC]
            c = gidx % NC
            if gidx >= 2:
                if gidx - 2 < NC:
                    self.w(e, "dve1", "fwqr", gidx - 2)
                else:
                    self.w(e, "casts", "cast", gidx - 2 - NC)
            dma(sb['wst'][:, gidx % 2, :], src[c * 128:(c + 1) * 128, :],
                "sww" + str(gidx % 2), key=("wchunk", gidx))
            if 1 <= gidx <= NC:
                cc = gidx - 1
                self.w(e, "dve1", "fwq", cc)
                dma(g['scr_wq'].ap()[cc * 128:(cc + 1) * 128, :], sb['xqt'][:, 0, :],
                    "sfq", key=("fwqw", cc))
        for m in range(NM):
            p = m % 2
            self.w(e, "dve1", "kevac", m)
            self.w(e, "dve1", "vdone", m)
            if not dry:
                for sec in range(2):
                    guard = e.If_eq(pr, 0) if sec == 0 else e.Else()
                    with guard:
                        e.dma_start(
                            out=g['sh_k'].ap()[sec, :, m, :, :],
                            in_=sb['kst'][:, :, :]).then_inc(self.S["shk"], 16)
                        e.dma_start(
                            out=g['sh_v'].ap()[sec, m * 512:(m + 1) * 512, :]
                            .rearrange("(s p) d -> p s d", p=128),
                            in_=sb['vst'][:, :, :]).then_inc(self.S["shv"], 16)
            self.cnt["shk"] = self.cnt.get("shk", 0) + 16
            self.cnt["shv"] = self.cnt.get("shv", 0) + 16
            self.marks[("shk", "shk", m)] = self.cnt["shk"]
            self.marks[("shv", "shv", m)] = self.cnt["shv"]
        self.mark("shk", "all")
        self.mark("shv", "all")

        self.w(e, "gb1", "b1", NM - 1)
        self.w(e, "pool1", "basedone", NM - 1)
        self.w(e, "pe1", "fmm", NM - 1)
        self.w(e, "dve1", "fstt", NM - 1, 1)
        self.w(e, "dve1", "tqev", NM - 1, 3)

        for i, (gg, m) in enumerate(self.jobs):
            if i >= 2:
                self.w(e, "pesc", "scj", i - 2)
            self.w(e, "gq2", "q2", m)
            dma(sb['qb'][:, i % 2, :, :], g['scr_q'].ap()[m, :, :, :],
                "sqj" + str(i % 2), key=("q", i))

    def em_gpsimd(self, e):
        cfg, sb, g = self.cfg, self.sb, self.g
        NM, NC, D = cfg.NM, cfg.NC, cfg.D
        dry = self.dry

        def dma(out, in_, sem, key=None, **kw):
            ins = None if dry else e.dma_start(out=out, in_=in_, **kw)
            self.inc(ins, sem, 16, key)

        pid = par = othr = pr2 = None
        if not dry:
            pid = e.partition_id()
            par = e.compute_val(pid % 2)
            othr = e.compute_val(1 - (pid % 2))
            pr2 = e.to_reg(pid % 2)

        dma(sb['rows'][0:1, 0:D], g['fb_in'], "grow", key=("rows_fb",))
        if not dry:
            with self.nc.allow_non_contiguous_dma(reason="one-time 128x4B bias column loads"):
                for f in range(NC):
                    e.dma_start(out=sb['bq'][:, f:f + 1],
                                in_=g['bq_in'][0:1, f * 128:(f + 1) * 128]
                                .rearrange("o (p q) -> (o p) q", q=1)
                                ).then_inc(self.S["gp0"], 16)
                    e.dma_start(out=sb['bk'][:, f:f + 1],
                                in_=g['bk_in'][0:1, f * 128:(f + 1) * 128]
                                .rearrange("o (p q) -> (o p) q", q=1)
                                ).then_inc(self.S["gp0"], 16)
        self.cnt["gp0"] = self.cnt.get("gp0", 0) + 32 * NC
        self.mark("gp0", "bias")
        self.w(e, "pe0", "rep", 0, cfg.NH - 1)
        dma(sb['rows'][0:1, 0:D], g['bv_in'], "grow", key=("rows_bv",))
        self.w(e, "sx0", "x", 0)
        self.w(e, "dve1", "repc", 0, cfg.NH - 1)
        for ts_ in range(2):
            self.op(lambda ts_=ts_: e.tensor_tensor(
                out=sb['bst'][:, ts_, :], in0=sb['x'][:, 0, ts_, :],
                in1=sb['fbr'][:, :], op=ALU.add),
                "pool1", key=("bstxf", 0) if ts_ == 1 else None)
        self.w(e, "dve1", "wscd")
        for f in range(NC):
            dma(g['scr_vec'].ap()[0:1, f * 128:(f + 1) * 128], sb['wsc'][:, f, 1:2],
                "gp0", key=("wsvw", f))
        self.w(e, "gp0", "wsvw", NC - 1)
        self.w(e, "pe0", "rep", 1, cfg.NH - 1)
        dma(sb['rows'][0:1, 0:D], g['scr_vec'].ap()[0:1, :], "grow",
            key=("rows_ws",))

        for m in range(NM):
            p = m % 2
            self.w(e, "dve1", "qevac", m)
            dma(g['scr_q'].ap()[m, :, :, :], sb['qst'][:, :, :], "gq2",
                key=("q2", m))
            for ts_ in range(2):
                self.w(e, "dve1", "fstt", m, ts_)
                self.op(lambda ts_=ts_: e.tensor_tensor(
                    out=sb['bst'][:, ts_, :], in0=sb['bst'][:, ts_, :],
                    in1=sb['t0'][:, ts_, :], op=ALU.add),
                    "pool1", key=("basedone", m) if ts_ == 1 else None)
            self.w(e, "pool1", "basedone", m)
            dma(g['scr_base'].ap()[m * 256:(m + 1) * 256, :]
                .rearrange("(s p) d -> p s d", p=128),
                sb['bst'][:, :, :], "gb1", key=("b1", m))
            if m + 1 < NM:
                self.w(e, "gb1", "b1", m)
                self.w(e, "sx" + str((m + 1) % 2), "x", m + 1)
                for ts_ in range(2):
                    self.op(lambda ts_=ts_, m=m: e.tensor_tensor(
                        out=sb['bst'][:, ts_, :],
                        in0=sb['x'][:, (m + 1) % 2, ts_, :],
                        in1=sb['fbr'][:, :], op=ALU.add),
                        "pool1", key=("bstxf", m + 1) if ts_ == 1 else None)
            if m == 0:
                self.w(e, "dve1", "ms")
            self.w(e, "shk", "shk", m)
            self.w(e, "shv", "shv", m)
            if not dry:
                for sec in range(2):
                    guard = e.If_eq(pr2, 0) if sec == 0 else e.Else()
                    with guard:
                        e.dma_start(out=g['sh_fl'].ap()[0, 2 * m + sec:
                                                        2 * m + sec + 1],
                                    in_=sb['flag'][0:1, 0:1]
                                    ).then_inc(self.S["gp0"], 16)
            self.cnt["gp0"] = self.cnt.get("gp0", 0) + 16
            self.mark("gp0", "kvflag", m)

        self.w(e, "gb1", "b1", NM - 1)
        self.w(e, "pool1", "basedone", NM - 1)
        self.w(e, "pe1", "fmm", NM - 1)
        self.w(e, "dve1", "fstt", NM - 1, 1)
        self.w(e, "dve1", "tqev", NM - 1, 3)

        KSN = self.cfg.KSN
        kvm = None
        if not dry:
            kvm = self.nc.monotonic_semaphore(0, engine=e)
        for gg in range(NM):
            gp = gg % 2
            if gg >= 2:
                self.w(e, "peav", "jdone", self.last_job_of_g[gg - 2])
            if gg % 4 == 0 and not dry:
                ps_ = min(gg + 3, NM - 1)
                for cell in (2 * ps_, 2 * ps_ + 1):
                    def cond(cell=cell):
                        e.dma_start(out=sb['poll'][0:1, 32:33],
                                    in_=g['sh_fl'].ap()[0, cell:cell + 1]
                                    ).then_inc(kvm.sem(), 16)
                        kvm.inc_expected(16)
                        kvm.wait()
                        return e.value_load(sb['poll'][0:1, 32:33]) != 1
                    with e.While(cond):
                        pass
            sname = "skv" + str(gg % 2)
            vname = "svb" + str(gg % 2)
            for rr in range(2):
                dma(sb['kb'][:, gp, rr, :, :],
                    g['sh_k'].ap()[rr, :, gg, :, :],
                    sname, key=None)
            self.mark(sname, "gk", gg)
            for ks in range(KSN):
                rr, kk = ks // 4, ks % 4
                dma(sb['vb'][:, gp, ks, :],
                    g['sh_v'].ap()[rr, gg * 512 + kk * 128: gg * 512 + (kk + 1) * 128, :],
                    vname, key=None)
            self.mark(vname, "g", gg)

        self.w(e, "dve0", "zme")
        swv = self.cnt["gp0"]
        if not dry:
            e.dma_start(out=g['sh_z'].ap()[bass.ts(par, 1), :],
                        in_=sb['zme'][:, :]).then_inc(self.S["gp0"], 16)
            e.wait_ge(self.S["gp0"], swv + 16)
            p2r = e.compute_val(2 + (pid % 2))
            e.dma_start(out=g['sh_fl'].ap()[bass.ts(p2r, 1), :],
                        in_=sb['flag'][:, :]).then_inc(self.S["gp0"], 16)
            e.wait_ge(self.S["gp0"], swv + 32)
            o2r = e.compute_val(3 - (pid % 2))
            msem2 = self.nc.monotonic_semaphore(1, engine=e)

            def cond2():
                e.dma_start(out=sb['poll'][0:1, 8:16],
                            in_=g['sh_fl'].ap()[bass.ts(o2r, 1), 0:8]
                            ).then_inc(msem2.sem(), 16)
                msem2.inc_expected(16)
                msem2.wait()
                return e.value_load(sb['poll'][0:1, 8:9]) != 1

            with e.While(cond2):
                pass
            e.dma_start(out=sb['zoth'][:, :],
                        in_=g['sh_z'].ap()[bass.ts(othr, 1), :]
                        ).then_inc(self.S["gp0"], 16)
            e.wait_ge(self.S["gp0"], swv + 48)
            e.memset(sb['poll'][0:1, 4:6], 0).then_inc(self.S["zbar"], 1)
        self.cnt["gp0"] = swv + 48
        self.cnt["zbar"] = self.cnt.get("zbar", 0) + 1

    def em_scalar(self, e):
        cfg, sb, g, ps = self.cfg, self.sb, self.g, self.ps
        NM, NC, KSN, D, NH = cfg.NM, cfg.NC, cfg.KSN, cfg.D, cfg.NH
        dry = self.dry

        def dma(out, in_, sem, key=None, **kw):
            ins = None if dry else e.dma_start(out=out, in_=in_, **kw)
            self.inc(ins, sem, 16, key)

        EPSC = sb['cst'][:, 0:1]
        MGC = sb['cst'][:, 1:2]
        ZERC = sb['cst'][:, 2:3]

        def load_x(m):
            if m >= 2:
                self.w(e, "dve1", "amaxs", m - 2)
                self.w(e, "pool1", "bstxf", m - 2)
            dma(sb['x'][:, m % 2, :, :],
                g['x_in'][m * 512:(m + 1) * 512, :]
                .rearrange("(s p) d -> p s d", p=128),
                "sx" + str(m % 2), key=("x", m))

        def mvstats(k):
            pk = k % 2
            st4 = sb['st'][:, pk, :, :]
            self.w(e, "sx" + str(pk), "x", k)
            if k >= 1:
                self.w(e, "pe1", "trmm", k - 1, NC - 1)
            if k >= 2:
                self.w(e, "dve1", "s3", k - 2)
            for t in range(4):
                self.op(lambda t=t, pk=pk, st4=st4: e.activation(
                    out=sb['xnt'][:, t, :], in_=sb['x'][:, pk, t, :],
                    func=AF.Copy, bias=0.0, scale=1.0,
                    accum_out=st4[:, t, 0:1]), "act1")
                self.op(lambda t=t, pk=pk, st4=st4: e.activation(
                    out=sb['xnt'][:, t, :], in_=sb['x'][:, pk, t, :],
                    func=AF.Square, bias=0.0, scale=1.0,
                    accum_out=st4[:, t, 1:2]), "act1")
            self.ch(e, "act1")
            self.op(lambda st4=st4: e.activation(
                out=st4[:, :, 2:3], in_=st4[:, :, 0:1],
                func=AF.Copy, bias=0.0, scale=1.0 / D), "act1")
            self.op(lambda st4=st4: e.activation(
                out=st4[:, :, 8:9], in_=st4[:, :, 0:1],
                func=AF.Copy, bias=0.0, scale=-1.0 / D), "act1")
            self.op(lambda st4=st4: e.activation(
                out=st4[:, :, 9:10], in_=st4[:, :, 1:2],
                func=AF.Copy, bias=0.0, scale=1.0 / D), "act1")
            self.ch(e, "act1")
            for t in range(4):
                self.op(lambda t=t, st4=st4: e.activation(
                    out=st4[:, t, 3:4], in_=st4[:, t, 2:3],
                    func=AF.Identity, bias=st4[:, t, 9:10],
                    scale=st4[:, t, 8:9]), "act1")
            self.ch(e, "act1")
            self.op(lambda st4=st4: e.activation(
                out=st4[:, :, 4:5], in_=st4[:, :, 3:4],
                func=AF.Ln, bias=EPSC, scale=1.0), "act1")
            self.ch(e, "act1")
            self.op(lambda st4=st4: e.activation(
                out=st4[:, :, 4:5], in_=st4[:, :, 4:5],
                func=AF.Exp, bias=0.0, scale=-0.5), "act1", key=("rstd", k))
            self.ch(e, "act1")
            for t in range(4):
                self.op(lambda t=t, st4=st4: e.activation(
                    out=st4[:, t, 5:6], in_=st4[:, t, 8:9],
                    func=AF.Identity, bias=ZERC, scale=st4[:, t, 4:5]), "act1")
            self.ch(e, "act1")
            for t in range(4):
                self.op(lambda t=t, pk=pk, st4=st4: e.activation(
                    out=sb['xnt'][:, t, :], in_=sb['x'][:, pk, t, :],
                    func=AF.Identity, bias=st4[:, t, 5:6],
                    scale=st4[:, t, 4:5]),
                    "act1", key=("xn", k) if t == 3 else None)

        load_x(0)
        if NM > 1:
            load_x(1)
        mvstats(0)
        self.w(e, "sfq", "fwqw", NC - 1)
        for c in range(NC):
            dma(sb['fwt'][:, c, :], g['scr_wq'].ap()[:, c * 128:(c + 1) * 128],
                "sfwt", key=("fwt", c), transpose=True)
        for m in range(NM):
            p = m % 2
            st4 = sb['st'][:, p, :, :]
            self.w(e, "dve1", "s3", m)
            if m == 0:
                self.w(e, "dve1", "fwq", NC - 1)
            if m >= 1:
                self.w(e, "pool1", "basedone", m - 1)
            for t in range(2):
                self.ch(e, "act1")
                self.op(lambda t=t, p=p, st4=st4: e.activation(
                    out=sb['scr'][:, p, :], in_=sb['x'][:, p, t, :],
                    func=AF.Identity, bias=st4[:, t, 7:8], scale=st4[:, t, 6:7]),
                    "act1")
                self.ch(e, "act1")
                self.op(lambda t=t, p=p: e.activation(
                    out=sb['t0'][:, t, :], in_=sb['scr'][:, p, :],
                    func=AF.Identity, bias=MGC, scale=1.0),
                    "act1", key=("magic", m, t))
            if m + 1 < NM:
                mvstats(m + 1)
            if m + 2 < NM:
                load_x(m + 2)


        self.w(e, "gb1", "b1", NM - 1)
        self.w(e, "pool1", "basedone", NM - 1)
        self.w(e, "pe1", "fmm", NM - 1)
        dma(sb['mkb'][:, :, :], g['msk_in'][:, :, :], "smk0", key=("mk",))
        for i, (gg, m) in enumerate(self.jobs):
            p2 = i % 2
            for ks in range(KSN):
                self.w(e, "pesc", "sc", i, ks)
                if i >= 2:
                    self.w(e, "peav", "jdone", i - 2)
                zc = i * KSN + ks
                acc = None
                if gg != m and not dry:
                    acc = sb['zacc'][:, zc:zc + 1]
                self.op(lambda ks=ks, acc=acc, p2=p2: e.activation(
                    out=sb['et'][:, p2, ks, :], in_=ps['a' + str(ks % 2)][:, :],
                    func=AF.Exp, bias=0.0, scale=1.0 / 32.0, accum_out=acc),
                    "aexp", key=("exp", i, ks))

        for o in range(2 * NM):
            m, su = o // 2, o % 2
            if o >= 2:
                self.w(e, "dve0", "out", o - 2)
            self.w(e, "gb1", "b1", m)
            dma(sb['bb'][:, o % 2, :],
                g['scr_base'].ap()[m * 256 + su * 128: m * 256 + (su + 1) * 128, :],
                "sbase" + str(o % 2), key=("bb", o))
            if o >= 2:
                oo = o - 2
                self.w(e, "dve0", "out", oo)
                dma(g['out_t'][(oo // 2) * 256 + (oo % 2) * 128:
                               (oo // 2) * 256 + (oo % 2 + 1) * 128, :],
                    sb['ob'][:, oo % 2, :], "sout" + str(oo % 2), key=("out", oo))
        for oo in range(max(0, 2 * NM - 2), 2 * NM):
            self.w(e, "dve0", "out", oo)
            dma(g['out_t'][(oo // 2) * 256 + (oo % 2) * 128:
                           (oo // 2) * 256 + (oo % 2 + 1) * 128, :],
                sb['ob'][:, oo % 2, :], "sout" + str(oo % 2), key=("out", oo))
        self.wv(e, "sout0", self.cnt.get("sout0", 0))
        self.wv(e, "sout1", self.cnt.get("sout1", 0))

    def em_vector(self, e):
        cfg, sb, g, ps = self.cfg, self.sb, self.g, self.ps
        NM, NC, KSN, D, NH = cfg.NM, cfg.NC, cfg.KSN, cfg.D, cfg.NH
        dry = self.dry
        op = self.op

        op(lambda: e.memset(sb['ones'][:, :], 1.0))
        op(lambda: e.memset(sb['cst'][:, 0:1], 1e-5))
        op(lambda: e.memset(sb['zme'][:, :], 0.0))
        op(lambda: e.memset(sb['zoth'][:, :], 0.0))
        op(lambda: e.memset(sb['alph'][:, :], 0.0))
        op(lambda: e.memset(sb['poll'][:, :], 0))
        op(lambda: e.memset(sb['cst'][:, 1:2], MAGIC))
        op(lambda: e.memset(sb['cst'][:, 2:3], 0.0))
        op(lambda: e.memset(sb['onesrow'][:, :], 1.0))
        op(lambda: e.memset(sb['flag'][:, :], 1), "dve1", key=("ms",))
        for j in range(2):
            for h in range(NH):
                self.w(e, "pe0", "rep", j, h)
                nm = ("fbr", "bvr")[j]
                op(lambda nm=nm, h=h: e.tensor_copy(
                    out=sb[nm][:, h * 512:(h + 1) * 512],
                    in_=ps['b0'][:, h * 512:(h + 1) * 512]),
                    "dve1", key=("repc", j, h))

        for c in range(NC):
            self.w(e, "sww" + str(c % 2), "wchunk", c)
            if c >= 1:
                self.w(e, "sfq", "fwqw", c - 1)
            src = sb['wst'][:, c % 2, :]
            op(lambda src=src, c=c: e.tensor_reduce(
                out=sb['wsc'][:, c, 0:1], in_=src, axis=AX.X, op=ALU.max,
                apply_absolute_value=True), "dve1")
            self.ch(e, "dve1")
            op(lambda c=c: e.tensor_scalar(out=sb['wsc'][:, c, 1:2],
                                           in0=sb['wsc'][:, c, 0:1],
                                           scalar1=1.0 / 127.0, scalar2=None,
                                           op0=ALU.mult), "dve1")
            op(lambda c=c: e.reciprocal(out=sb['wsc'][:, c, 2:3],
                                        in_=sb['wsc'][:, c, 0:1]), "dve1")
            self.ch(e, "dve1")
            op(lambda c=c: e.tensor_scalar(out=sb['wsc'][:, c, 2:3],
                                           in0=sb['wsc'][:, c, 2:3],
                                           scalar1=127.0, scalar2=None,
                                           op0=ALU.mult), "dve1")
            self.ch(e, "dve1")
            op(lambda src=src, c=c: e.tensor_scalar(
                out=sb['t0'][:, 0, :], in0=src, scalar1=sb['wsc'][:, c, 2:3],
                scalar2=MAGIC, op0=ALU.mult, op1=ALU.add),
                "dve1", key=("fwqr", c))
            self.ch(e, "dve1")
            op(lambda c=c: e.tensor_scalar(
                out=sb['xqt'][:, 0, :], in0=sb['t0'][:, 0, :],
                scalar1=MAGIC, scalar2=None, op0=ALU.subtract),
                "dve1", key=("fwq", c))
        self.mark("dve1", "wscd")

        def p1_iter(m):
            p = m % 2
            st4 = sb['st'][:, p, :, :]
            am2 = sb['amx'][:, p, :, :]
            self.w(e, "sx" + str(m % 2), "x", m)
            self.w(e, "act1", "rstd", m)
            if m >= 2:
                self.w(e, "act1", "magic", m - 2, 1)
            for t in range(2):
                self.ch(e, "dve1")
                op(lambda t=t, p=p, st4=st4: e.tensor_scalar(
                    out=sb['scr'][:, p, :], in0=sb['x'][:, p, t, :],
                    scalar1=st4[:, t, 2:3], scalar2=None, op0=ALU.subtract), "dve1")
                self.ch(e, "dve1")
                op(lambda t=t, p=p, am2=am2: e.tensor_reduce(
                    out=am2[:, t, 0:1], in_=sb['scr'][:, p, :], axis=AX.X,
                    op=ALU.max, apply_absolute_value=True),
                    "dve1", key=("amaxs", m) if t == 1 else None)
            self.ch(e, "dve1")
            op(lambda am2=am2, st4=st4: e.tensor_tensor(
                out=am2[:, :, 1:2], in0=am2[:, :, 0:1],
                in1=st4[:, 0:2, 4:5], op=ALU.mult), "dve1")
            self.ch(e, "dve1")
            op(lambda am2=am2: e.reciprocal(out=am2[:, :, 2:3], in_=am2[:, :, 1:2]), "dve1")
            self.ch(e, "dve1")
            op(lambda am2=am2, st4=st4: e.scalar_tensor_tensor(
                out=st4[:, 0:2, 6:7], in0=am2[:, :, 2:3],
                scalar=127.0, in1=st4[:, 0:2, 4:5],
                op0=ALU.mult, op1=ALU.mult), "dve1")
            self.ch(e, "dve1")
            op(lambda st4=st4: e.scalar_tensor_tensor(
                out=st4[:, 0:2, 7:8], in0=st4[:, 0:2, 2:3],
                scalar=-1.0, in1=st4[:, 0:2, 6:7],
                op0=ALU.mult, op1=ALU.mult),
               "dve1", key=("s3", m))
            op(lambda am2=am2: e.tensor_scalar(
                out=am2[:, :, 3:4], in0=am2[:, :, 1:2],
                scalar1=1.0 / 127.0, scalar2=None, op0=ALU.mult),
               "dve1", key=("xsc", m))
            if m == 0:
                self.emit_trev(e, 0)
                for j in range(3 * NC):
                    nm = ("wq", "wk", "wv")[j // NC]
                    gidx = NC + j
                    self.w(e, "sww" + str(gidx % 2), "wchunk", gidx)
                    op(lambda nm=nm, j=j, gidx=gidx: e.tensor_copy(
                        out=sb[nm][:, j % NC, :], in_=sb['wst'][:, gidx % 2, :]),
                        "casts", key=("cast", j))
            else:
                self.emit_vtq_evacs(e, m - 1)
            self.emit_qk_evacs(e, m)
            if m == 0:
                for h in range(self.cfg.NH):
                    self.w(e, "pe0", "rep", 2, h)
                    op(lambda h=h: e.tensor_copy(
                        out=sb['wsr'][:, h * 512:(h + 1) * 512],
                        in_=ps['b0'][:, h * 512:(h + 1) * 512]),
                        "dve1", key=("repc", 2, h))
                self.mark("dve1", "reps")
            if m + 1 < NM:
                self.emit_trev(e, m + 1)
            for t in range(2):
                self.w(e, "act1", "magic", m, t)
                if m == 0 and t == 0:
                    self.w(e, "sfq", "fwqw", NC - 1)
                if m >= 1 and t == 0:
                    self.w(e, "pe1", "tqmm", m - 1, 3)
                op(lambda t=t: e.tensor_scalar(
                    out=sb['xqt'][:, t, :], in0=sb['t0'][:, t, :],
                    scalar1=MAGIC, scalar2=None, op0=ALU.subtract),
                    "dve1", key=("xq", m) if t == 1 else None)

        for m in range(NM):
            p1_iter(m)
        self.emit_vtq_evacs(e, NM - 1)

        for i, (gg, m) in enumerate(self.jobs):
            p2 = i % 2
            if gg == m:
                self.w(e, "smk0", "mk")
                for ks in range(KSN):
                    zc = i * KSN + ks
                    self.w(e, "aexp", "exp", i, ks)
                    op(lambda ks=ks, p2=p2, gg=gg: e.tensor_tensor(
                        out=sb['et'][:, p2, ks, :], in0=sb['et'][:, p2, ks, :],
                        in1=sb['mkb'][:, ks, :], op=ALU.mult), "dve1")
                    self.ch(e, "dve1")
                    op(lambda ks=ks, zc=zc, p2=p2: e.reduce_sum(
                        out=sb['zacc'][:, zc:zc + 1], in_=sb['et'][:, p2, ks, :],
                        axis=AX.X), "dmask", key=("dm", i, ks))
                self.mark("dmask", "mdone", gg)
            for su in range(2):
                self.w(e, "peav", "avs", i, su)
                if gg == 0:
                    op(lambda su=su, m=m: e.tensor_copy(
                        out=sb['usb'][:, m, su, :], in_=ps['b' + str(su)][:, :]),
                        "dveu", key=("ud", i, su))
                else:
                    op(lambda su=su, m=m: e.tensor_tensor(
                        out=sb['usb'][:, m, su, :], in0=ps['b' + str(su)][:, :],
                        in1=sb['usb'][:, m, su, :], op=ALU.add),
                        "dveu", key=("ud", i, su))
            self.mark("dveu", "udone", i)

        self.wv(e, "aexp", self.NJOB * KSN)
        self.wv(e, "dmask", NM * KSN)
        op(lambda: e.reduce_sum(out=sb['alph'][:, 0:1],
                                in_=sb['zacc'][:, 0:self.ZCOLS], axis=AX.X),
           "dve0", key=("zred",))
        self.w(e, "pe0", "zmm")
        op(lambda: e.tensor_copy(out=sb['zme'][0:1, 0:1], in_=ps['a0'][0:1, 0:1]),
           "dve0", key=("zme",))
        self.wv(e, "zbar", 1)
        self.ch(e, "dve0")
        op(lambda: e.tensor_tensor(out=sb['zme'][0:1, 1:2], in0=sb['zme'][0:1, 0:1],
                                   in1=sb['zoth'][0:1, 0:1], op=ALU.add),
           "dve0", key=("ztot",))
        self.w(e, "pe0", "zrep")
        op(lambda: e.reciprocal(out=sb['alph'][:, 1:2], in_=ps['a0'][:, 2:3]),
           "dve0", key=("alph",))
        self.ch(e, "dve0")
        for o in range(2 * NM):
            m, su = o // 2, o % 2
            self.w(e, "sbase" + str(o % 2), "bb", o)
            if o >= 2:
                self.w(e, "sout" + str(o % 2), "out", o - 2)
            op(lambda m=m, su=su, o=o: e.scalar_tensor_tensor(
                out=sb['ob'][:, o % 2, :], in0=sb['usb'][:, m, su, :],
                scalar=sb['alph'][:, 1:2], in1=sb['bb'][:, o % 2, :],
                op0=ALU.mult, op1=ALU.add), "dve0", key=("out", o))

    def emit_trev(self, e, k):
        sb, ps, NC = self.sb, self.ps, self.cfg.NC
        for b in range(NC):
            reg, _prev = self.ttinfo[("tr", k, b)]
            self.w(e, "pe1", "trmm", k, b)
            self.op(lambda b=b, k=k, reg=reg: e.tensor_copy(
                out=sb['xnf'][:, k % 2, b, :],
                in_=ps['pt'][:, reg * 1024:reg * 1024 + 512]),
                "dve1", key=("trev", k, b))

    def emit_qk_evacs(self, e, s):
        sb, ps, NC = self.sb, self.ps, self.cfg.NC
        op = self.op
        if s == 0:
            self.w(e, "gp0", "wsvw", NC - 1)
        if s >= 1:
            self.w(e, "gq2", "q2", s - 1)
        for fo in range(NC):
            self.w(e, "pe1", "qmm", s, fo)
            op(lambda fo=fo: e.tensor_scalar(
                out=sb['qst'][:, fo, :], in0=ps['a' + str(fo % 2)][:, :],
                scalar1=sb['bq'][:, fo:fo + 1], scalar2=None, op0=ALU.add),
                "dve1", key=("qev", s, fo))
        self.mark("dve1", "qevac", s)
        if s >= 1:
            self.w(e, "shk", "shk", s - 1)
        for fo in range(NC):
            self.w(e, "pe1", "kmm", s, fo)
            op(lambda fo=fo: e.tensor_scalar(
                out=sb['kst'][:, fo, :], in0=ps['a' + str(fo % 2)][:, :],
                scalar1=sb['bk'][:, fo:fo + 1], scalar2=None, op0=ALU.add),
                "dve1", key=("kev", s, fo))
        self.mark("dve1", "kevac", s)

    def emit_vtq_evacs(self, e, s):
        sb, ps, NC = self.sb, self.ps, self.cfg.NC
        op = self.op
        am2 = sb['amx'][:, s % 2, :, :]
        if s >= 1:
            self.w(e, "shv", "shv", s - 1)
        else:
            self.w(e, "casts", "cast", 3 * self.cfg.NC - 1)
        for ts_ in range(4):
            self.w(e, "pe1", "vmmt", s, ts_)
            op(lambda ts_=ts_: e.tensor_tensor(
                out=sb['vst'][:, ts_, :], in0=ps['b' + str(ts_ % 2)][:, :],
                in1=sb['bvr'][:, :], op=ALU.add),
                "dve1", key=("vevac", s, ts_))
            reg, _prev = self.ttinfo[("tq", s, ts_)]
            self.w(e, "pe1", "tqmm", s, ts_)
            op(lambda ts_=ts_, s=s, reg=reg: e.tensor_copy(
                out=sb['xqf'][:, s % 2, 2 * ts_:2 * ts_ + 2, :],
                in_=ps['pt'][:, reg * 1024:reg * 1024 + 512]),
                "dve1", key=("tqev", s, ts_))
        self.mark("dve1", "vdone", s)
        for ts_ in range(2):
            self.w(e, "pe1", "fmmt", s, ts_)
            op(lambda ts_=ts_, am2=am2: e.scalar_tensor_tensor(
                out=sb['t0'][:, ts_, :], in0=ps['b' + str(ts_ % 2)][:, :],
                scalar=am2[:, ts_, 3:4], in1=sb['wsr'][:, :],
                op0=ALU.mult, op1=ALU.mult), "dve1", key=("fstt", s, ts_))

    def em_tensor(self, e):
        cfg, sb, ps = self.cfg, self.sb, self.ps
        NM, NC, KSN, D, NH = cfg.NM, cfg.NC, cfg.KSN, cfg.D, cfg.NH
        dry = self.dry

        def mm(out, lhsT, rhs, start, stop, sem=None, key=None):
            ins = None
            if not dry:
                ins = e.matmul(out, lhsT, rhs, start=start, stop=stop)
            if sem is not None:
                self.inc(ins, sem, 1, key)

        self.w(e, "dve1", "ms")
        seq = 0
        for j in range(2):
            self.w(e, "grow", ("rows_fb", "rows_bv")[j])
            for h in range(NH):
                if seq > 0:
                    pj, phh = divmod(seq - 1, NH)
                    self.w(e, "dve1", "repc", pj, phh)
                mm(ps['b0'][:, h * 512:(h + 1) * 512], sb['onesrow'][0:1, :],
                   sb['rows'][0:1, h * 512:(h + 1) * 512],
                   True, True, "pe0", key=("rep", j, h))
                seq += 1

        def tpose(u, srcs_):
            reg, prev = self.ttinfo[u]
            if prev is not None:
                self.w(e, "dve1", "trev" if prev[0] == "tr" else "tqev",
                       prev[1], prev[2])
            ins = None
            for q, ap in enumerate(srcs_):
                if not dry:
                    ins = e.transpose(
                        out=ps['pt'][:, reg * 1024 + q * 128:
                                     reg * 1024 + (q + 1) * 128],
                        in_=ap, identity=sb['idm'][:, :])
            self.inc(ins, "pe1", 1, key=(u[0] + "mm", u[1], u[2]))

        self.wv(e, "sw", 32)
        self.w(e, "act1", "xn", 0)
        for b in range(NC):
            tpose(("tr", 0, b),
                  [sb['xnt'][:, t, b * 128:(b + 1) * 128] for t in range(4)])

        for m in range(NM):
            p = m % 2
            if m == 0:
                self.w(e, "casts", "cast", NC - 1)
            self.w(e, "dve1", "trev", m, NC - 1)
            for which in ("q", "k"):
                wname = 'wq' if which == "q" else 'wk'
                if which == "k" and m == 0:
                    self.w(e, "casts", "cast", 2 * NC - 1)
                for fo in range(NC):
                    if which == "q":
                        if fo >= 2:
                            self.w(e, "dve1", "qev", m, fo - 2)
                        elif m > 0:
                            self.w(e, "dve1", "kev", m - 1, NC - 2 + fo)
                    else:
                        if fo >= 2:
                            self.w(e, "dve1", "kev", m, fo - 2)
                        else:
                            self.w(e, "dve1", "qev", m, NC - 2 + fo)
                    for c in range(NC):
                        mm(ps['a' + str(fo % 2)][:, :],
                           sb[wname][:, c, fo * 128:(fo + 1) * 128],
                           sb['xnf'][:, p, c, :], c == 0, c == NC - 1,
                           "pe1" if c == NC - 1 else None,
                           key=(which + "mm", m, fo) if c == NC - 1 else None)
            if m == 0:
                self.w(e, "grow", "rows_ws")
                for h in range(NH):
                    pj, phh = (1, NH - 1) if h == 0 else (2, h - 1)
                    self.w(e, "dve1", "repc", pj, phh)
                    mm(ps['b0'][:, h * 512:(h + 1) * 512], sb['onesrow'][0:1, :],
                       sb['rows'][0:1, h * 512:(h + 1) * 512],
                       True, True, "pe0", key=("rep", 2, h))
            if m + 1 < NM:
                self.w(e, "act1", "xn", m + 1)
                for b in range(NC):
                    tpose(("tr", m + 1, b),
                          [sb['xnt'][:, t, b * 128:(b + 1) * 128]
                           for t in range(4)])
            if m == 0:
                self.w(e, "casts", "cast", 3 * NC - 1)
                self.w(e, "dve1", "reps")
            for ts_ in range(4):
                if ts_ >= 2:
                    self.w(e, "dve1", "vevac", m, ts_ - 2)
                elif m > 0:
                    self.w(e, "dve1", "fstt", m - 1, ts_)
                for c in range(NC):
                    for h in range(NH):
                        last = (c == NC - 1 and h == NH - 1)
                        mm(ps['b' + str(ts_ % 2)][:, h * 512:(h + 1) * 512],
                           sb['xnf'][:, p, c, ts_ * 128:(ts_ + 1) * 128],
                           sb['wv'][:, c, h * 512:(h + 1) * 512],
                           c == 0, c == NC - 1,
                           "pe1" if last else None,
                           key=("vmmt", m, ts_) if last else None)
                if ts_ == 0:
                    self.w(e, "dve1", "xq", m)
                tpose(("tq", m, ts_),
                      [sb['xqt'][:, t, (2 * ts_ + cc) * 128:
                                 (2 * ts_ + cc + 1) * 128]
                       for cc in range(2) for t in range(2)])
            self.mark("pe1", "vmm", m)
            if m == 0:
                self.w(e, "sfwt", "fwt", NC - 1)
            self.w(e, "dve1", "tqev", m, 3)
            for ts_ in range(2):
                self.w(e, "dve1", "vevac", m, 2 + ts_)
                for c in range(NC):
                    for h in range(NH):
                        last = (c == NC - 1 and h == NH - 1)
                        mm(ps['b' + str(ts_ % 2)][:, h * 512:(h + 1) * 512],
                           sb['xqf'][:, p, c, ts_ * 128:(ts_ + 1) * 128],
                           sb['fwt'][:, c, h * 512:(h + 1) * 512],
                           c == 0, c == NC - 1,
                           "pe1" if last else None,
                           key=("fmmt", m, ts_) if last else None)
            self.mark("pe1", "fmm", m)

        def av_mms(i, gg, m):
            p2 = i % 2
            self.w(e, "svb" + str(gg % 2), "g", gg)
            self.wv(e, "dveu", 2 * i)
            for su in range(2):
                for ks in range(KSN):
                    if gg == m:
                        self.w(e, "dmask", "dm", i, ks)
                    else:
                        self.w(e, "aexp", "exp", i, ks)
                    for h in range(NH):
                        last = (ks == KSN - 1 and h == NH - 1)
                        mm(ps['b' + str(su)][:, h * 512:(h + 1) * 512],
                           sb['et'][:, p2, ks, su * 128:(su + 1) * 128],
                           sb['vb'][:, gg % 2, ks, h * 512:(h + 1) * 512],
                           ks == 0, ks == KSN - 1,
                           "peav" if last else None,
                           key=("avs", i, su) if last else None)
            self.mark("peav", "jdone", i)

        for i, (gg, m) in enumerate(self.jobs):
            self.w(e, "skv" + str(gg % 2), "gk", gg)
            self.w(e, "sqj" + str(i % 2), "q", i)
            for ks in range(KSN):
                if ks >= 2:
                    self.w(e, "aexp", "exp", i, ks - 2)
                elif i > 0:
                    self.w(e, "aexp", "exp", i - 1, KSN - 2 + ks)
                for c in range(NC):
                    last = c == NC - 1
                    mm(ps['a' + str(ks % 2)][:, :],
                       sb['kb'][:, gg % 2, ks // 4, c,
                                (ks % 4) * 128:(ks % 4 + 1) * 128],
                       sb['qb'][:, i % 2, c, :], c == 0, last,
                       "pesc" if last else None,
                       key=("sc", i, ks) if last else None)
            self.mark("pesc", "scj", i)
            av_mms(i, gg, m)

        self.w(e, "dve0", "zred")
        mm(ps['a0'][0:1, 0:1], sb['alph'][:, 0:1], sb['ones'][:, 0:1], True, True,
           "pe0", key=("zmm",))
        self.wv(e, "zbar", 1)
        self.w(e, "dve0", "ztot")
        mm(ps['a0'][:, 2:3], sb['onesrow'][0:1, :], sb['zme'][0:1, 1:2], True, True,
           "pe0", key=("zrep",))


def build_program(cfg):
    b = Builder(cfg)
    nc = b.build()
    return nc, b




def kernel(L0_attn_x, L0_attn_Wq, L0_attn_bq, L0_attn_Wk, L0_attn_bk,
           L0_attn_Wv, L0_attn_bv, L0_ffn_weight, L0_ffn_bias, _cfg=None):
    cfg = _cfg or Cfg(T=16384, D=1024)
    x = np.ascontiguousarray(np.asarray(L0_attn_x, dtype=np.float32))
    sched = schedule(cfg)
    nc, _b = build_program(cfg)

    in_maps = []
    for c in range(8):
        bat = c // 4
        per_m = sched[c]
        rows = []
        for (t_av, t_so) in per_m:
            rows.append(x[bat, t_av * 256:(t_av + 1) * 256, :])
            rows.append(x[bat, t_so * 256:(t_so + 1) * 256, :])
        x_shard = np.ascontiguousarray(np.concatenate(rows, axis=0))
        tbl = np.zeros((1, max(2 * cfg.NM, 16)), dtype=np.int32)
        for m, (t_av, t_so) in enumerate(per_m):
            tbl[0, 2 * m] = t_av
            tbl[0, 2 * m + 1] = t_so
        in_maps.append({
            "x_in": x_shard,
            "wq_in": np.ascontiguousarray(np.asarray(L0_attn_Wq, dtype=np.float32)),
            "wk_in": np.ascontiguousarray(np.asarray(L0_attn_Wk, dtype=np.float32)),
            "wv_in": np.ascontiguousarray(np.asarray(L0_attn_Wv, dtype=np.float32)),
            "fw_in": np.ascontiguousarray(np.asarray(L0_ffn_weight, dtype=np.float32)),
            "bq_in": np.asarray(L0_attn_bq, dtype=np.float32).reshape(1, -1),
            "bk_in": np.asarray(L0_attn_bk, dtype=np.float32).reshape(1, -1),
            "bv_in": np.asarray(L0_attn_bv, dtype=np.float32).reshape(1, -1),
            "fb_in": np.asarray(L0_ffn_bias, dtype=np.float32).reshape(1, -1),
            "msk_in": np.ascontiguousarray(host_masks(cfg, sched, c)[0]),
            "id_in": np.eye(128, dtype=ml_dtypes.bfloat16),
            "tbl_in": tbl,
        })

    res = run_bass_kernel_spmd(nc, in_maps, core_ids=list(range(8)))

    out = np.zeros((cfg.B, cfg.T, cfg.D), dtype=np.float32)
    for c in range(8):
        bat = c // 4
        o = np.asarray(res.results[c]["out_t"])
        for m, (t_av, _t_so) in enumerate(sched[c]):
            out[bat, t_av * 256:(t_av + 1) * 256, :] = o[m * 256:(m + 1) * 256, :]
    return out


# version 17
# speedup vs baseline: 1.0437x; 1.0437x over previous
import numpy as np
import ml_dtypes
import concourse.bass as bass
import concourse.mybir as mybir
from concourse.bass_utils import run_bass_kernel_spmd

F32 = mybir.dt.float32
BF16 = mybir.dt.bfloat16
I32 = mybir.dt.int32
AF = mybir.ActivationFunctionType
ALU = mybir.AluOpType
AX = mybir.AxisListType

MAGIC = 12582912.0


class Cfg:
    def __init__(self, T=16384, D=1024, detect_races=False):
        self.detect_races = detect_races
        self.B = 2
        self.T = T
        self.D = D
        self.QT = 256
        self.KG = 1024
        self.NC = D // 128
        self.NH = D // 512
        self.NTILES = T // self.QT
        self.NM = self.NTILES // 4
        self.KSN = self.KG // 128
        self.PTOK = self.NM * 512
        self.OTOK = self.NM * 256
        assert self.NM * self.KG == T


def schedule(cfg):
    out = []
    for c in range(8):
        i = (c >> 1) & 1
        r = c & 1
        per_m = []
        for m in range(cfg.NM):
            t4 = [4 * m + j for j in range(4)]
            if i == 0:
                per_m.append((t4[r], t4[2 + r]))
            else:
                per_m.append((t4[2 + r], t4[r]))
        out.append(per_m)
    return out


def host_masks(cfg, sched, c):
    per_m = sched[c]
    base = (c // 2) * 2
    quads = []
    for m in range(cfg.NM):
        a0, s0 = sched[base][m]
        a1, s1 = sched[base + 1][m]
        quads.append([a0, s0, a1, s1])
    M = np.zeros((cfg.NM, cfg.KSN, 128, 512), dtype=np.float32)
    for m, (t_av, t_so) in enumerate(per_m):
        for ks in range(cfg.KSN):
            sec_tile = quads[m][ks // 2]
            k = sec_tile * cfg.QT + (ks % 2) * 128 + np.arange(128)[:, None]
            qa = t_av * cfg.QT + np.arange(256)[None, :]
            qs = t_so * cfg.QT + np.arange(256)[None, :]
            M[m, ks, :, 0:256] = (qa >= k)
            M[m, ks, :, 256:512] = (qs >= k)
    return M.astype(ml_dtypes.bfloat16)


class Builder:
    def __init__(self, cfg):
        self.cfg = cfg
        self.dry = True
        self.marks = {}
        self.cnt = {}

    def inc(self, ins, sem, n=1, key=None):
        self.cnt[sem] = self.cnt.get(sem, 0) + n
        if key is not None:
            self.marks[(sem,) + tuple(key)] = self.cnt[sem]
        if not self.dry and ins is not None:
            ins.then_inc(self.S[sem], n)

    def mark(self, sem, *key):
        self.marks[(sem,) + tuple(key)] = self.cnt.get(sem, 0)

    def w(self, e, sem, *key):
        if not self.dry:
            e.wait_ge(self.S[sem], self.marks[(sem,) + tuple(key)])

    def ch(self, e, sem):
        if not self.dry and self.cnt.get(sem, 0) > 0:
            e.wait_ge(self.S[sem], self.cnt[sem])

    def wv(self, e, sem, val):
        if not self.dry and val > 0:
            e.wait_ge(self.S[sem], val)

    def op(self, fn, sem=None, n=1, key=None):
        ins = None
        if not self.dry:
            ins = fn()
        if sem is not None:
            self.inc(ins, sem, n, key)
        elif key is not None:
            raise ValueError("key without sem")

    def build(self):
        cfg = self.cfg
        T, D, NC, NM, KSN = cfg.T, cfg.D, cfg.NC, cfg.NM, cfg.KSN
        nc = self.nc = bass.Bass("TRN2", target_bir_lowering=False,
                                 detect_race_conditions=self.cfg.detect_races)

        g = self.g = {}
        g['x_in'] = nc.dram_tensor("x_in", [cfg.PTOK, D], F32, kind="ExternalInput").ap()
        for nm in ("wq_in", "wk_in", "wv_in", "fw_in"):
            g[nm] = nc.dram_tensor(nm, [D, D], F32, kind="ExternalInput").ap()
        for nm in ("bq_in", "bk_in", "bv_in", "fb_in"):
            g[nm] = nc.dram_tensor(nm, [1, D], F32, kind="ExternalInput").ap()
        g['msk_in'] = nc.dram_tensor("msk_in", [KSN, 128, 512], BF16,
                                     kind="ExternalInput").ap()
        g['id_in'] = nc.dram_tensor("id_in", [128, 128], BF16,
                                    kind="ExternalInput").ap()
        g['tbl_in'] = nc.dram_tensor("tbl_in", [1, max(2 * NM, 16)], I32,
                                     kind="ExternalInput").ap()
        g['out_t'] = nc.dram_tensor("out_t", [cfg.OTOK, D], F32,
                                    kind="ExternalOutput").ap()

        g['scr_wq'] = nc.dram_tensor("scr_wq", [D, D], BF16)
        g['scr_q'] = nc.dram_tensor("scr_q", [NM, 128, NC, 512], BF16)
        g['scr_base'] = nc.dram_tensor("scr_base", [cfg.OTOK, D], F32)
        g['scr_vec'] = nc.dram_tensor("scr_vec", [1, D], F32)

        g['sh_k'] = nc.dram_tensor("sh_k", [2, 128, NM, NC, 512], BF16, addr_space="Shared")
        g['sh_v'] = nc.dram_tensor("sh_v", [2, T // 2, D], BF16, addr_space="Shared")
        g['sh_z'] = nc.dram_tensor("sh_z", [2, 128], F32, addr_space="Shared")
        g['sh_fl'] = nc.dram_tensor("sh_fl", [4, 128], I32, addr_space="Shared")

        seqlist = [("tr", 0, b) for b in range(NC)]
        for m_ in range(NM):
            if m_ + 1 < NM:
                seqlist += [("tr", m_ + 1, b) for b in range(NC)]
            seqlist += [("tq", m_, b) for b in range(4)]
        self.ttinfo = {}
        for i_, u_ in enumerate(seqlist):
            self.ttinfo[u_] = (i_ % 2, seqlist[i_ - 2] if i_ >= 2 else None)

        self.jobs = [(gg, m) for gg in range(NM) for m in range(gg, NM)]
        self.NJOB = len(self.jobs)
        self.ZCOLS = self.NJOB * KSN
        self.last_job_of_g = {}
        for i, (gg, m) in enumerate(self.jobs):
            self.last_job_of_g[gg] = i

        from contextlib import ExitStack
        with ExitStack() as ctx:
            E = ctx.enter_context
            sb = self.sb = {}

            def S_(name, shape, dt, EE=None):
                sb[name] = (EE or E)(nc.sbuf_tensor(name, shape, dt))

            S_("bq", [128, NC], F32)
            S_("bk", [128, NC], F32)
            S_("wsc", [128, NC, 4], F32)
            S_("ones", [128, 2], F32)
            S_("onesrow", [1, 128], F32)
            S_("tbl", [1, max(2 * NM, 16)], I32)
            S_("zacc", [128, max(self.ZCOLS, 4)], F32)
            S_("flag", [1, 128], I32)
            S_("poll", [1, 128], I32)
            S_("zme", [1, 128], F32)
            S_("zoth", [1, 128], F32)
            S_("alph", [128, 4], F32)
            S_("cst", [128, 4], F32)
            S_("idm", [128, 128], BF16)
            p1_stack = ExitStack()
            E1 = p1_stack.enter_context
            S_("bvr", [128, D], F32, E1)
            S_("fbr", [128, D], F32, E1)
            S_("wsr", [128, D], F32, E1)
            S_("wq", [128, NC, D], BF16, E1)
            S_("wk", [128, NC, D], BF16, E1)
            S_("wv", [128, NC, D], BF16, E1)
            S_("fwt", [128, NC, D], BF16, E1)
            S_("rows", [1, D], F32, E1)
            S_("x", [128, 2, 4, D], F32, E1)
            S_("xnf", [128, 2, NC, 512], BF16, E1)
            S_("xqf", [128, 2, NC, 256], BF16, E1)
            S_("xnt", [128, 4, D], BF16, E1)
            S_("xqt", [128, 2, D], BF16, E1)
            S_("scr", [128, 2, D], F32, E1)
            S_("t0", [128, 2, D], F32, E1)
            S_("st", [128, 2, 4, 12], F32, E1)
            S_("amx", [128, 2, 2, 4], F32, E1)
            S_("qst", [128, NC, 512], BF16, E1)
            S_("kst", [128, NC, 512], BF16, E1)
            p0_stack = ExitStack()
            S_("wst", [128, 2, D], F32, p0_stack.enter_context)
            p0_stack.close()
            S_("vst", [128, 4, D], BF16, E1)
            S_("bst", [128, 2, D], F32, E1)
            p1_stack.close()
            S_("usb", [128, NM, 2, D], BF16)
            S_("kb", [128, 2, 2, NC, 512], BF16)
            S_("vb", [128, 2, KSN, D], BF16)
            S_("qb", [128, 2, NC, 512], BF16)
            S_("et", [128, 2, KSN, 512], BF16)
            S_("mkb", [128, KSN, 512], BF16)
            S_("ob", [128, 2, D], F32)
            S_("bb", [128, 4, D], F32)

            ps = self.ps = {}
            ps['a0'] = E(nc.psum_tensor("ps_a0", [128, 512], F32))
            ps['a1'] = E(nc.psum_tensor("ps_a1", [128, 512], F32))
            ps['b0'] = E(nc.psum_tensor("ps_b0", [128, D], F32))
            ps['b1'] = E(nc.psum_tensor("ps_b1", [128, D], F32))
            ps['pt'] = E(nc.psum_tensor("ps_pt", [128, 2 * D], BF16))

            sems = ["sx0", "sx1", "sw", "sww0", "sww1", "sfq",
                    "sfwt", "pe1", "act1",
                    "dve1", "pool1", "shk", "shv", "gp0", "gq2", "gb1", "bar",
                    "skv0", "skv1", "svb0", "svb1", "sqj0", "sqj1", "smk0", "smk1", "pesc",
                    "aexp", "dmask", "peav", "dveu", "zbar", "sbase0",
                    "sbase1", "sbase2", "sbase3", "sout0", "sout1", "pe0",
                    "dve0", "casts", "grow"]
            self.S = {n: E(nc.semaphore(n)) for n in sems}
            block = E(nc.Block())

            self.dry = True
            self.cnt = {}
            for f in (self.em_sync, self.em_gpsimd, self.em_scalar, self.em_vector,
                      self.em_tensor):
                f(None)
            dry_cnt = dict(self.cnt)
            self.dry = False
            self.cnt = {}
            block.sync(self.em_sync)
            block.gpsimd(self.em_gpsimd)
            block.scalar(self.em_scalar)
            block.vector(self.em_vector)
            block.tensor(self.em_tensor)
            assert dry_cnt == self.cnt, (dry_cnt, self.cnt)
        return nc

    def em_sync(self, e):
        cfg, sb, g = self.cfg, self.sb, self.g
        NM, NC, KSN, KG = cfg.NM, cfg.NC, cfg.KSN, cfg.KG
        dry = self.dry

        def dma(out, in_, sem, key=None, **kw):
            ins = None if dry else e.dma_start(out=out, in_=in_, **kw)
            self.inc(ins, sem, 16, key)

        dma(sb['tbl'][:, :], g['tbl_in'], "sw", key=("tbl",))
        dma(sb['idm'][:, :], g['id_in'], "sw", key=("idm",))
        pr = None
        if not dry:
            pid = e.partition_id()
            pr = e.to_reg(pid % 2)

        srcs = (g['fw_in'], g['wq_in'], g['wk_in'], g['wv_in'])
        for gidx in range(4 * NC):
            src = srcs[gidx // NC]
            c = gidx % NC
            if gidx >= 2:
                if gidx - 2 < NC:
                    self.w(e, "dve1", "fwqr", gidx - 2)
                else:
                    self.w(e, "casts", "cast", gidx - 2 - NC)
            dma(sb['wst'][:, gidx % 2, :], src[c * 128:(c + 1) * 128, :],
                "sww" + str(gidx % 2), key=("wchunk", gidx))
            if 1 <= gidx <= NC:
                cc = gidx - 1
                self.w(e, "dve1", "fwq", cc)
                dma(g['scr_wq'].ap()[cc * 128:(cc + 1) * 128, :], sb['xqt'][:, 0, :],
                    "sfq", key=("fwqw", cc))
        for m in range(NM):
            p = m % 2
            self.w(e, "dve1", "kevac", m)
            self.w(e, "dve1", "vdone", m)
            if not dry:
                for sec in range(2):
                    guard = e.If_eq(pr, 0) if sec == 0 else e.Else()
                    with guard:
                        e.dma_start(
                            out=g['sh_k'].ap()[sec, :, m, :, :],
                            in_=sb['kst'][:, :, :]).then_inc(self.S["shk"], 16)
                        e.dma_start(
                            out=g['sh_v'].ap()[sec, m * 512:(m + 1) * 512, :]
                            .rearrange("(s p) d -> p s d", p=128),
                            in_=sb['vst'][:, :, :]).then_inc(self.S["shv"], 16)
            self.cnt["shk"] = self.cnt.get("shk", 0) + 16
            self.cnt["shv"] = self.cnt.get("shv", 0) + 16
            self.marks[("shk", "shk", m)] = self.cnt["shk"]
            self.marks[("shv", "shv", m)] = self.cnt["shv"]
        self.mark("shk", "all")
        self.mark("shv", "all")

        self.w(e, "gb1", "b1", NM - 1)
        self.w(e, "pool1", "basedone", NM - 1)
        self.w(e, "pe1", "fmm", NM - 1)
        self.w(e, "dve1", "fstt", NM - 1, 1)
        self.w(e, "dve1", "tqev", NM - 1, 3)

        for i, (gg, m) in enumerate(self.jobs):
            if i >= 2:
                self.w(e, "pesc", "scj", i - 2)
            self.w(e, "gq2", "q2", m)
            dma(sb['qb'][:, i % 2, :, :], g['scr_q'].ap()[m, :, :, :],
                "sqj" + str(i % 2), key=("q", i))

    def em_gpsimd(self, e):
        cfg, sb, g = self.cfg, self.sb, self.g
        NM, NC, D = cfg.NM, cfg.NC, cfg.D
        dry = self.dry

        def dma(out, in_, sem, key=None, **kw):
            ins = None if dry else e.dma_start(out=out, in_=in_, **kw)
            self.inc(ins, sem, 16, key)

        pid = par = othr = pr2 = None
        if not dry:
            pid = e.partition_id()
            par = e.compute_val(pid % 2)
            othr = e.compute_val(1 - (pid % 2))
            pr2 = e.to_reg(pid % 2)

        dma(sb['rows'][0:1, 0:D], g['fb_in'], "grow", key=("rows_fb",))
        if not dry:
            with self.nc.allow_non_contiguous_dma(reason="one-time 128x4B bias column loads"):
                for f in range(NC):
                    e.dma_start(out=sb['bq'][:, f:f + 1],
                                in_=g['bq_in'][0:1, f * 128:(f + 1) * 128]
                                .rearrange("o (p q) -> (o p) q", q=1)
                                ).then_inc(self.S["gp0"], 16)
                    e.dma_start(out=sb['bk'][:, f:f + 1],
                                in_=g['bk_in'][0:1, f * 128:(f + 1) * 128]
                                .rearrange("o (p q) -> (o p) q", q=1)
                                ).then_inc(self.S["gp0"], 16)
        self.cnt["gp0"] = self.cnt.get("gp0", 0) + 32 * NC
        self.mark("gp0", "bias")
        self.w(e, "pe0", "rep", 0, cfg.NH - 1)
        dma(sb['rows'][0:1, 0:D], g['bv_in'], "grow", key=("rows_bv",))
        self.w(e, "sx0", "x", 0)
        self.w(e, "dve1", "repc", 0, cfg.NH - 1)
        for ts_ in range(2):
            self.op(lambda ts_=ts_: e.tensor_tensor(
                out=sb['bst'][:, ts_, :], in0=sb['x'][:, 0, ts_, :],
                in1=sb['fbr'][:, :], op=ALU.add),
                "pool1", key=("bstxf", 0) if ts_ == 1 else None)
        self.w(e, "dve1", "wscd")
        for f in range(NC):
            dma(g['scr_vec'].ap()[0:1, f * 128:(f + 1) * 128], sb['wsc'][:, f, 1:2],
                "gp0", key=("wsvw", f))
        self.w(e, "gp0", "wsvw", NC - 1)
        self.w(e, "pe0", "rep", 1, cfg.NH - 1)
        dma(sb['rows'][0:1, 0:D], g['scr_vec'].ap()[0:1, :], "grow",
            key=("rows_ws",))

        for m in range(NM):
            p = m % 2
            self.w(e, "dve1", "qevac", m)
            dma(g['scr_q'].ap()[m, :, :, :], sb['qst'][:, :, :], "gq2",
                key=("q2", m))
            for ts_ in range(2):
                self.w(e, "dve1", "fstt", m, ts_)
                self.op(lambda ts_=ts_: e.tensor_tensor(
                    out=sb['bst'][:, ts_, :], in0=sb['bst'][:, ts_, :],
                    in1=sb['t0'][:, ts_, :], op=ALU.add),
                    "pool1", key=("basedone", m) if ts_ == 1 else None)
            self.w(e, "pool1", "basedone", m)
            dma(g['scr_base'].ap()[m * 256:(m + 1) * 256, :]
                .rearrange("(s p) d -> p s d", p=128),
                sb['bst'][:, :, :], "gb1", key=("b1", m))
            if m + 1 < NM:
                self.w(e, "gb1", "b1", m)
                self.w(e, "sx" + str((m + 1) % 2), "x", m + 1)
                for ts_ in range(2):
                    self.op(lambda ts_=ts_, m=m: e.tensor_tensor(
                        out=sb['bst'][:, ts_, :],
                        in0=sb['x'][:, (m + 1) % 2, ts_, :],
                        in1=sb['fbr'][:, :], op=ALU.add),
                        "pool1", key=("bstxf", m + 1) if ts_ == 1 else None)
            if m == 0:
                self.w(e, "dve1", "ms")
            self.w(e, "shk", "shk", m)
            self.w(e, "shv", "shv", m)
            if not dry:
                for sec in range(2):
                    guard = e.If_eq(pr2, 0) if sec == 0 else e.Else()
                    with guard:
                        e.dma_start(out=g['sh_fl'].ap()[0, 2 * m + sec:
                                                        2 * m + sec + 1],
                                    in_=sb['flag'][0:1, 0:1]
                                    ).then_inc(self.S["gp0"], 16)
            self.cnt["gp0"] = self.cnt.get("gp0", 0) + 16
            self.mark("gp0", "kvflag", m)

        self.w(e, "gb1", "b1", NM - 1)
        self.w(e, "pool1", "basedone", NM - 1)
        self.w(e, "pe1", "fmm", NM - 1)
        self.w(e, "dve1", "fstt", NM - 1, 1)
        self.w(e, "dve1", "tqev", NM - 1, 3)

        KSN = self.cfg.KSN
        kvm = None
        if not dry:
            kvm = self.nc.monotonic_semaphore(0, engine=e)
        for gg in range(NM):
            gp = gg % 2
            if gg >= 2:
                self.w(e, "peav", "jdone", self.last_job_of_g[gg - 2])
            if gg % 4 == 0 and not dry:
                ps_ = min(gg + 3, NM - 1)
                for cell in (2 * ps_, 2 * ps_ + 1):
                    def cond(cell=cell):
                        e.dma_start(out=sb['poll'][0:1, 32:33],
                                    in_=g['sh_fl'].ap()[0, cell:cell + 1]
                                    ).then_inc(kvm.sem(), 16)
                        kvm.inc_expected(16)
                        kvm.wait()
                        return e.value_load(sb['poll'][0:1, 32:33]) != 1
                    with e.While(cond):
                        pass
            sname = "skv" + str(gg % 2)
            vname = "svb" + str(gg % 2)
            for rr in range(2):
                dma(sb['kb'][:, gp, rr, :, :],
                    g['sh_k'].ap()[rr, :, gg, :, :],
                    sname, key=None)
            self.mark(sname, "gk", gg)
            for ks in range(KSN):
                rr, kk = ks // 4, ks % 4
                dma(sb['vb'][:, gp, ks, :],
                    g['sh_v'].ap()[rr, gg * 512 + kk * 128: gg * 512 + (kk + 1) * 128, :],
                    vname, key=None)
            self.mark(vname, "g", gg)

        self.w(e, "dve0", "zme")
        swv = self.cnt["gp0"]
        if not dry:
            e.dma_start(out=g['sh_z'].ap()[bass.ts(par, 1), :],
                        in_=sb['zme'][:, :]).then_inc(self.S["gp0"], 16)
            e.wait_ge(self.S["gp0"], swv + 16)
            p2r = e.compute_val(2 + (pid % 2))
            e.dma_start(out=g['sh_fl'].ap()[bass.ts(p2r, 1), :],
                        in_=sb['flag'][:, :]).then_inc(self.S["gp0"], 16)
            e.wait_ge(self.S["gp0"], swv + 32)
            o2r = e.compute_val(3 - (pid % 2))
            msem2 = self.nc.monotonic_semaphore(1, engine=e)

            def cond2():
                e.dma_start(out=sb['poll'][0:1, 8:16],
                            in_=g['sh_fl'].ap()[bass.ts(o2r, 1), 0:8]
                            ).then_inc(msem2.sem(), 16)
                msem2.inc_expected(16)
                msem2.wait()
                return e.value_load(sb['poll'][0:1, 8:9]) != 1

            with e.While(cond2):
                pass
            e.dma_start(out=sb['zoth'][:, :],
                        in_=g['sh_z'].ap()[bass.ts(othr, 1), :]
                        ).then_inc(self.S["gp0"], 16)
            e.wait_ge(self.S["gp0"], swv + 48)
            e.memset(sb['poll'][0:1, 4:6], 0).then_inc(self.S["zbar"], 1)
        self.cnt["gp0"] = swv + 48
        self.cnt["zbar"] = self.cnt.get("zbar", 0) + 1

    def em_scalar(self, e):
        cfg, sb, g, ps = self.cfg, self.sb, self.g, self.ps
        NM, NC, KSN, D, NH = cfg.NM, cfg.NC, cfg.KSN, cfg.D, cfg.NH
        dry = self.dry

        def dma(out, in_, sem, key=None, **kw):
            ins = None if dry else e.dma_start(out=out, in_=in_, **kw)
            self.inc(ins, sem, 16, key)

        EPSC = sb['cst'][:, 0:1]
        MGC = sb['cst'][:, 1:2]
        ZERC = sb['cst'][:, 2:3]

        def load_x(m):
            if m >= 2:
                self.w(e, "dve1", "amaxs", m - 2)
                self.w(e, "pool1", "bstxf", m - 2)
            dma(sb['x'][:, m % 2, :, :],
                g['x_in'][m * 512:(m + 1) * 512, :]
                .rearrange("(s p) d -> p s d", p=128),
                "sx" + str(m % 2), key=("x", m))

        def mvstats(k):
            pk = k % 2
            st4 = sb['st'][:, pk, :, :]
            self.w(e, "sx" + str(pk), "x", k)
            if k >= 1:
                self.w(e, "pe1", "trmm", k - 1, NC - 1)
            if k >= 2:
                self.w(e, "dve1", "s3", k - 2)
            for t in range(4):
                self.op(lambda t=t, pk=pk, st4=st4: e.activation(
                    out=sb['xnt'][:, t, :], in_=sb['x'][:, pk, t, :],
                    func=AF.Copy, bias=0.0, scale=1.0,
                    accum_out=st4[:, t, 0:1]), "act1")
                self.op(lambda t=t, pk=pk, st4=st4: e.activation(
                    out=sb['xnt'][:, t, :], in_=sb['x'][:, pk, t, :],
                    func=AF.Square, bias=0.0, scale=1.0,
                    accum_out=st4[:, t, 1:2]), "act1")
            self.ch(e, "act1")
            self.op(lambda st4=st4: e.activation(
                out=st4[:, :, 2:3], in_=st4[:, :, 0:1],
                func=AF.Copy, bias=0.0, scale=1.0 / D), "act1")
            self.op(lambda st4=st4: e.activation(
                out=st4[:, :, 8:9], in_=st4[:, :, 0:1],
                func=AF.Copy, bias=0.0, scale=-1.0 / D), "act1")
            self.op(lambda st4=st4: e.activation(
                out=st4[:, :, 9:10], in_=st4[:, :, 1:2],
                func=AF.Copy, bias=0.0, scale=1.0 / D), "act1")
            self.ch(e, "act1")
            for t in range(4):
                self.op(lambda t=t, st4=st4: e.activation(
                    out=st4[:, t, 3:4], in_=st4[:, t, 2:3],
                    func=AF.Identity, bias=st4[:, t, 9:10],
                    scale=st4[:, t, 8:9]), "act1")
            self.ch(e, "act1")
            self.op(lambda st4=st4: e.activation(
                out=st4[:, :, 4:5], in_=st4[:, :, 3:4],
                func=AF.Ln, bias=EPSC, scale=1.0), "act1")
            self.ch(e, "act1")
            self.op(lambda st4=st4: e.activation(
                out=st4[:, :, 4:5], in_=st4[:, :, 4:5],
                func=AF.Exp, bias=0.0, scale=-0.5), "act1", key=("rstd", k))
            self.ch(e, "act1")
            for t in range(4):
                self.op(lambda t=t, st4=st4: e.activation(
                    out=st4[:, t, 5:6], in_=st4[:, t, 8:9],
                    func=AF.Identity, bias=ZERC, scale=st4[:, t, 4:5]), "act1")
            self.ch(e, "act1")
            for t in range(4):
                self.op(lambda t=t, pk=pk, st4=st4: e.activation(
                    out=sb['xnt'][:, t, :], in_=sb['x'][:, pk, t, :],
                    func=AF.Identity, bias=st4[:, t, 5:6],
                    scale=st4[:, t, 4:5]),
                    "act1", key=("xn", k) if t == 3 else None)

        load_x(0)
        if NM > 1:
            load_x(1)
        mvstats(0)
        self.w(e, "sfq", "fwqw", NC - 1)
        for c in range(NC):
            dma(sb['fwt'][:, c, :], g['scr_wq'].ap()[:, c * 128:(c + 1) * 128],
                "sfwt", key=("fwt", c), transpose=True)
        for m in range(NM):
            p = m % 2
            st4 = sb['st'][:, p, :, :]
            self.w(e, "dve1", "s3", m)
            if m == 0:
                self.w(e, "dve1", "fwq", NC - 1)
            if m >= 1:
                self.w(e, "pool1", "basedone", m - 1)
            for t in range(2):
                self.ch(e, "act1")
                self.op(lambda t=t, p=p, st4=st4: e.activation(
                    out=sb['scr'][:, p, :], in_=sb['x'][:, p, t, :],
                    func=AF.Identity, bias=st4[:, t, 7:8], scale=st4[:, t, 6:7]),
                    "act1")
                self.ch(e, "act1")
                self.op(lambda t=t, p=p: e.activation(
                    out=sb['t0'][:, t, :], in_=sb['scr'][:, p, :],
                    func=AF.Identity, bias=MGC, scale=1.0),
                    "act1", key=("magic", m, t))
            if m + 1 < NM:
                mvstats(m + 1)
            if m + 2 < NM:
                load_x(m + 2)


        self.w(e, "gb1", "b1", NM - 1)
        self.w(e, "pool1", "basedone", NM - 1)
        self.w(e, "pe1", "fmm", NM - 1)
        dma(sb['mkb'][:, :, :], g['msk_in'][:, :, :], "smk0", key=("mk",))
        for i, (gg, m) in enumerate(self.jobs):
            p2 = i % 2
            for ks in range(KSN):
                self.w(e, "pesc", "sc", i, ks)
                if i >= 2:
                    self.w(e, "peav", "jdone", i - 2)
                zc = i * KSN + ks
                acc = None
                if gg != m and not dry:
                    acc = sb['zacc'][:, zc:zc + 1]
                self.op(lambda ks=ks, acc=acc, p2=p2: e.activation(
                    out=sb['et'][:, p2, ks, :], in_=ps['a' + str(ks % 2)][:, :],
                    func=AF.Exp, bias=0.0, scale=1.0 / 32.0, accum_out=acc),
                    "aexp", key=("exp", i, ks))

        for o in range(2 * NM):
            m, su = o // 2, o % 2
            if o >= 4:
                self.w(e, "dve0", "out", o - 4)
            self.w(e, "gb1", "b1", m)
            dma(sb['bb'][:, o % 4, :],
                g['scr_base'].ap()[m * 256 + su * 128: m * 256 + (su + 1) * 128, :],
                "sbase" + str(o % 4), key=("bb", o))
            if o >= 2:
                oo = o - 2
                self.w(e, "dve0", "out", oo)
                dma(g['out_t'][(oo // 2) * 256 + (oo % 2) * 128:
                               (oo // 2) * 256 + (oo % 2 + 1) * 128, :],
                    sb['ob'][:, oo % 2, :], "sout" + str(oo % 2), key=("out", oo))
        for oo in range(max(0, 2 * NM - 2), 2 * NM):
            self.w(e, "dve0", "out", oo)
            dma(g['out_t'][(oo // 2) * 256 + (oo % 2) * 128:
                           (oo // 2) * 256 + (oo % 2 + 1) * 128, :],
                sb['ob'][:, oo % 2, :], "sout" + str(oo % 2), key=("out", oo))
        self.wv(e, "sout0", self.cnt.get("sout0", 0))
        self.wv(e, "sout1", self.cnt.get("sout1", 0))

    def em_vector(self, e):
        cfg, sb, g, ps = self.cfg, self.sb, self.g, self.ps
        NM, NC, KSN, D, NH = cfg.NM, cfg.NC, cfg.KSN, cfg.D, cfg.NH
        dry = self.dry
        op = self.op

        op(lambda: e.memset(sb['ones'][:, :], 1.0))
        op(lambda: e.memset(sb['cst'][:, 0:1], 1e-5))
        op(lambda: e.memset(sb['zme'][:, :], 0.0))
        op(lambda: e.memset(sb['zoth'][:, :], 0.0))
        op(lambda: e.memset(sb['alph'][:, :], 0.0))
        op(lambda: e.memset(sb['poll'][:, :], 0))
        op(lambda: e.memset(sb['cst'][:, 1:2], MAGIC))
        op(lambda: e.memset(sb['cst'][:, 2:3], 0.0))
        op(lambda: e.memset(sb['onesrow'][:, :], 1.0))
        op(lambda: e.memset(sb['flag'][:, :], 1), "dve1", key=("ms",))
        for j in range(2):
            for h in range(NH):
                self.w(e, "pe0", "rep", j, h)
                nm = ("fbr", "bvr")[j]
                op(lambda nm=nm, h=h: e.tensor_copy(
                    out=sb[nm][:, h * 512:(h + 1) * 512],
                    in_=ps['b0'][:, h * 512:(h + 1) * 512]),
                    "dve1", key=("repc", j, h))

        for c in range(NC):
            self.w(e, "sww" + str(c % 2), "wchunk", c)
            if c >= 1:
                self.w(e, "sfq", "fwqw", c - 1)
            src = sb['wst'][:, c % 2, :]
            op(lambda src=src, c=c: e.tensor_reduce(
                out=sb['wsc'][:, c, 0:1], in_=src, axis=AX.X, op=ALU.max,
                apply_absolute_value=True), "dve1")
            self.ch(e, "dve1")
            op(lambda c=c: e.tensor_scalar(out=sb['wsc'][:, c, 1:2],
                                           in0=sb['wsc'][:, c, 0:1],
                                           scalar1=1.0 / 127.0, scalar2=None,
                                           op0=ALU.mult), "dve1")
            op(lambda c=c: e.reciprocal(out=sb['wsc'][:, c, 2:3],
                                        in_=sb['wsc'][:, c, 0:1]), "dve1")
            self.ch(e, "dve1")
            op(lambda c=c: e.tensor_scalar(out=sb['wsc'][:, c, 2:3],
                                           in0=sb['wsc'][:, c, 2:3],
                                           scalar1=127.0, scalar2=None,
                                           op0=ALU.mult), "dve1")
            self.ch(e, "dve1")
            op(lambda src=src, c=c: e.tensor_scalar(
                out=sb['t0'][:, 0, :], in0=src, scalar1=sb['wsc'][:, c, 2:3],
                scalar2=MAGIC, op0=ALU.mult, op1=ALU.add),
                "dve1", key=("fwqr", c))
            self.ch(e, "dve1")
            op(lambda c=c: e.tensor_scalar(
                out=sb['xqt'][:, 0, :], in0=sb['t0'][:, 0, :],
                scalar1=MAGIC, scalar2=None, op0=ALU.subtract),
                "dve1", key=("fwq", c))
        self.mark("dve1", "wscd")

        def p1_iter(m):
            p = m % 2
            st4 = sb['st'][:, p, :, :]
            am2 = sb['amx'][:, p, :, :]
            self.w(e, "sx" + str(m % 2), "x", m)
            self.w(e, "act1", "rstd", m)
            if m >= 2:
                self.w(e, "act1", "magic", m - 2, 1)
            for t in range(2):
                self.ch(e, "dve1")
                op(lambda t=t, p=p, st4=st4: e.tensor_scalar(
                    out=sb['scr'][:, p, :], in0=sb['x'][:, p, t, :],
                    scalar1=st4[:, t, 2:3], scalar2=None, op0=ALU.subtract), "dve1")
                self.ch(e, "dve1")
                op(lambda t=t, p=p, am2=am2: e.tensor_reduce(
                    out=am2[:, t, 0:1], in_=sb['scr'][:, p, :], axis=AX.X,
                    op=ALU.max, apply_absolute_value=True),
                    "dve1", key=("amaxs", m) if t == 1 else None)
            self.ch(e, "dve1")
            op(lambda am2=am2, st4=st4: e.tensor_tensor(
                out=am2[:, :, 1:2], in0=am2[:, :, 0:1],
                in1=st4[:, 0:2, 4:5], op=ALU.mult), "dve1")
            self.ch(e, "dve1")
            op(lambda am2=am2: e.reciprocal(out=am2[:, :, 2:3], in_=am2[:, :, 1:2]), "dve1")
            self.ch(e, "dve1")
            op(lambda am2=am2, st4=st4: e.scalar_tensor_tensor(
                out=st4[:, 0:2, 6:7], in0=am2[:, :, 2:3],
                scalar=127.0, in1=st4[:, 0:2, 4:5],
                op0=ALU.mult, op1=ALU.mult), "dve1")
            self.ch(e, "dve1")
            op(lambda st4=st4: e.scalar_tensor_tensor(
                out=st4[:, 0:2, 7:8], in0=st4[:, 0:2, 2:3],
                scalar=-1.0, in1=st4[:, 0:2, 6:7],
                op0=ALU.mult, op1=ALU.mult),
               "dve1", key=("s3", m))
            op(lambda am2=am2: e.tensor_scalar(
                out=am2[:, :, 3:4], in0=am2[:, :, 1:2],
                scalar1=1.0 / 127.0, scalar2=None, op0=ALU.mult),
               "dve1", key=("xsc", m))
            if m == 0:
                self.emit_trev(e, 0)
                for j in range(3 * NC):
                    nm = ("wq", "wk", "wv")[j // NC]
                    gidx = NC + j
                    self.w(e, "sww" + str(gidx % 2), "wchunk", gidx)
                    op(lambda nm=nm, j=j, gidx=gidx: e.tensor_copy(
                        out=sb[nm][:, j % NC, :], in_=sb['wst'][:, gidx % 2, :]),
                        "casts", key=("cast", j))
            else:
                self.emit_vtq_evacs(e, m - 1)
            self.emit_qk_evacs(e, m)
            if m == 0:
                for h in range(self.cfg.NH):
                    self.w(e, "pe0", "rep", 2, h)
                    op(lambda h=h: e.tensor_copy(
                        out=sb['wsr'][:, h * 512:(h + 1) * 512],
                        in_=ps['b0'][:, h * 512:(h + 1) * 512]),
                        "dve1", key=("repc", 2, h))
                self.mark("dve1", "reps")
            if m + 1 < NM:
                self.emit_trev(e, m + 1)
            for t in range(2):
                self.w(e, "act1", "magic", m, t)
                if m == 0 and t == 0:
                    self.w(e, "sfq", "fwqw", NC - 1)
                if m >= 1 and t == 0:
                    self.w(e, "pe1", "tqmm", m - 1, 3)
                op(lambda t=t: e.tensor_scalar(
                    out=sb['xqt'][:, t, :], in0=sb['t0'][:, t, :],
                    scalar1=MAGIC, scalar2=None, op0=ALU.subtract),
                    "dve1", key=("xq", m) if t == 1 else None)

        for m in range(NM):
            p1_iter(m)
        self.emit_vtq_evacs(e, NM - 1)

        for i, (gg, m) in enumerate(self.jobs):
            p2 = i % 2
            if gg == m:
                self.w(e, "smk0", "mk")
                for ks in range(KSN):
                    zc = i * KSN + ks
                    self.w(e, "aexp", "exp", i, ks)
                    op(lambda ks=ks, p2=p2, gg=gg: e.tensor_tensor(
                        out=sb['et'][:, p2, ks, :], in0=sb['et'][:, p2, ks, :],
                        in1=sb['mkb'][:, ks, :], op=ALU.mult), "dve1")
                    self.ch(e, "dve1")
                    op(lambda ks=ks, zc=zc, p2=p2: e.reduce_sum(
                        out=sb['zacc'][:, zc:zc + 1], in_=sb['et'][:, p2, ks, :],
                        axis=AX.X), "dmask", key=("dm", i, ks))
                self.mark("dmask", "mdone", gg)
            for su in range(2):
                self.w(e, "peav", "avs", i, su)
                if gg == 0:
                    op(lambda su=su, m=m: e.tensor_copy(
                        out=sb['usb'][:, m, su, :], in_=ps['b' + str(su)][:, :]),
                        "dveu", key=("ud", i, su))
                else:
                    op(lambda su=su, m=m: e.tensor_tensor(
                        out=sb['usb'][:, m, su, :], in0=ps['b' + str(su)][:, :],
                        in1=sb['usb'][:, m, su, :], op=ALU.add),
                        "dveu", key=("ud", i, su))
            self.mark("dveu", "udone", i)

        self.wv(e, "aexp", self.NJOB * KSN)
        self.wv(e, "dmask", NM * KSN)
        op(lambda: e.reduce_sum(out=sb['alph'][:, 0:1],
                                in_=sb['zacc'][:, 0:self.ZCOLS], axis=AX.X),
           "dve0", key=("zred",))
        self.w(e, "pe0", "zmm")
        op(lambda: e.tensor_copy(out=sb['zme'][0:1, 0:1], in_=ps['a0'][0:1, 0:1]),
           "dve0", key=("zme",))
        self.wv(e, "zbar", 1)
        self.ch(e, "dve0")
        op(lambda: e.tensor_tensor(out=sb['zme'][0:1, 1:2], in0=sb['zme'][0:1, 0:1],
                                   in1=sb['zoth'][0:1, 0:1], op=ALU.add),
           "dve0", key=("ztot",))
        self.w(e, "pe0", "zrep")
        op(lambda: e.reciprocal(out=sb['alph'][:, 1:2], in_=ps['a0'][:, 2:3]),
           "dve0", key=("alph",))
        self.ch(e, "dve0")
        for o in range(2 * NM):
            m, su = o // 2, o % 2
            self.w(e, "sbase" + str(o % 4), "bb", o)
            if o >= 2:
                self.w(e, "sout" + str(o % 2), "out", o - 2)
            op(lambda m=m, su=su, o=o: e.scalar_tensor_tensor(
                out=sb['ob'][:, o % 2, :], in0=sb['usb'][:, m, su, :],
                scalar=sb['alph'][:, 1:2], in1=sb['bb'][:, o % 4, :],
                op0=ALU.mult, op1=ALU.add), "dve0", key=("out", o))

    def emit_trev(self, e, k):
        sb, ps, NC = self.sb, self.ps, self.cfg.NC
        for b in range(NC):
            reg, _prev = self.ttinfo[("tr", k, b)]
            self.w(e, "pe1", "trmm", k, b)
            self.op(lambda b=b, k=k, reg=reg: e.tensor_copy(
                out=sb['xnf'][:, k % 2, b, :],
                in_=ps['pt'][:, reg * 1024:reg * 1024 + 512]),
                "dve1", key=("trev", k, b))

    def emit_qk_evacs(self, e, s):
        sb, ps, NC = self.sb, self.ps, self.cfg.NC
        op = self.op
        if s == 0:
            self.w(e, "gp0", "wsvw", NC - 1)
        if s >= 1:
            self.w(e, "gq2", "q2", s - 1)
        for fo in range(NC):
            self.w(e, "pe1", "qmm", s, fo)
            op(lambda fo=fo: e.tensor_scalar(
                out=sb['qst'][:, fo, :], in0=ps['a' + str(fo % 2)][:, :],
                scalar1=sb['bq'][:, fo:fo + 1], scalar2=None, op0=ALU.add),
                "dve1", key=("qev", s, fo))
        self.mark("dve1", "qevac", s)
        if s >= 1:
            self.w(e, "shk", "shk", s - 1)
        for fo in range(NC):
            self.w(e, "pe1", "kmm", s, fo)
            op(lambda fo=fo: e.tensor_scalar(
                out=sb['kst'][:, fo, :], in0=ps['a' + str(fo % 2)][:, :],
                scalar1=sb['bk'][:, fo:fo + 1], scalar2=None, op0=ALU.add),
                "dve1", key=("kev", s, fo))
        self.mark("dve1", "kevac", s)

    def emit_vtq_evacs(self, e, s):
        sb, ps, NC = self.sb, self.ps, self.cfg.NC
        op = self.op
        am2 = sb['amx'][:, s % 2, :, :]
        if s >= 1:
            self.w(e, "shv", "shv", s - 1)
        else:
            self.w(e, "casts", "cast", 3 * self.cfg.NC - 1)
        for ts_ in range(4):
            self.w(e, "pe1", "vmmt", s, ts_)
            op(lambda ts_=ts_: e.tensor_tensor(
                out=sb['vst'][:, ts_, :], in0=ps['b' + str(ts_ % 2)][:, :],
                in1=sb['bvr'][:, :], op=ALU.add),
                "dve1", key=("vevac", s, ts_))
            reg, _prev = self.ttinfo[("tq", s, ts_)]
            self.w(e, "pe1", "tqmm", s, ts_)
            op(lambda ts_=ts_, s=s, reg=reg: e.tensor_copy(
                out=sb['xqf'][:, s % 2, 2 * ts_:2 * ts_ + 2, :],
                in_=ps['pt'][:, reg * 1024:reg * 1024 + 512]),
                "dve1", key=("tqev", s, ts_))
        self.mark("dve1", "vdone", s)
        for ts_ in range(2):
            self.w(e, "pe1", "fmmt", s, ts_)
            op(lambda ts_=ts_, am2=am2: e.scalar_tensor_tensor(
                out=sb['t0'][:, ts_, :], in0=ps['b' + str(ts_ % 2)][:, :],
                scalar=am2[:, ts_, 3:4], in1=sb['wsr'][:, :],
                op0=ALU.mult, op1=ALU.mult), "dve1", key=("fstt", s, ts_))

    def em_tensor(self, e):
        cfg, sb, ps = self.cfg, self.sb, self.ps
        NM, NC, KSN, D, NH = cfg.NM, cfg.NC, cfg.KSN, cfg.D, cfg.NH
        dry = self.dry

        def mm(out, lhsT, rhs, start, stop, sem=None, key=None):
            ins = None
            if not dry:
                ins = e.matmul(out, lhsT, rhs, start=start, stop=stop)
            if sem is not None:
                self.inc(ins, sem, 1, key)

        self.w(e, "dve1", "ms")
        seq = 0
        for j in range(2):
            self.w(e, "grow", ("rows_fb", "rows_bv")[j])
            for h in range(NH):
                if seq > 0:
                    pj, phh = divmod(seq - 1, NH)
                    self.w(e, "dve1", "repc", pj, phh)
                mm(ps['b0'][:, h * 512:(h + 1) * 512], sb['onesrow'][0:1, :],
                   sb['rows'][0:1, h * 512:(h + 1) * 512],
                   True, True, "pe0", key=("rep", j, h))
                seq += 1

        def tpose(u, srcs_):
            reg, prev = self.ttinfo[u]
            if prev is not None:
                self.w(e, "dve1", "trev" if prev[0] == "tr" else "tqev",
                       prev[1], prev[2])
            ins = None
            for q, ap in enumerate(srcs_):
                if not dry:
                    ins = e.transpose(
                        out=ps['pt'][:, reg * 1024 + q * 128:
                                     reg * 1024 + (q + 1) * 128],
                        in_=ap, identity=sb['idm'][:, :])
            self.inc(ins, "pe1", 1, key=(u[0] + "mm", u[1], u[2]))

        self.wv(e, "sw", 32)
        self.w(e, "act1", "xn", 0)
        for b in range(NC):
            tpose(("tr", 0, b),
                  [sb['xnt'][:, t, b * 128:(b + 1) * 128] for t in range(4)])

        for m in range(NM):
            p = m % 2
            if m == 0:
                self.w(e, "casts", "cast", NC - 1)
            self.w(e, "dve1", "trev", m, NC - 1)
            for which in ("q", "k"):
                wname = 'wq' if which == "q" else 'wk'
                if which == "k" and m == 0:
                    self.w(e, "casts", "cast", 2 * NC - 1)
                for fo in range(NC):
                    if which == "q":
                        if fo >= 2:
                            self.w(e, "dve1", "qev", m, fo - 2)
                        elif m > 0:
                            self.w(e, "dve1", "kev", m - 1, NC - 2 + fo)
                    else:
                        if fo >= 2:
                            self.w(e, "dve1", "kev", m, fo - 2)
                        else:
                            self.w(e, "dve1", "qev", m, NC - 2 + fo)
                    for c in range(NC):
                        mm(ps['a' + str(fo % 2)][:, :],
                           sb[wname][:, c, fo * 128:(fo + 1) * 128],
                           sb['xnf'][:, p, c, :], c == 0, c == NC - 1,
                           "pe1" if c == NC - 1 else None,
                           key=(which + "mm", m, fo) if c == NC - 1 else None)
            if m == 0:
                self.w(e, "grow", "rows_ws")
                for h in range(NH):
                    pj, phh = (1, NH - 1) if h == 0 else (2, h - 1)
                    self.w(e, "dve1", "repc", pj, phh)
                    mm(ps['b0'][:, h * 512:(h + 1) * 512], sb['onesrow'][0:1, :],
                       sb['rows'][0:1, h * 512:(h + 1) * 512],
                       True, True, "pe0", key=("rep", 2, h))
            if m + 1 < NM:
                self.w(e, "act1", "xn", m + 1)
                for b in range(NC):
                    tpose(("tr", m + 1, b),
                          [sb['xnt'][:, t, b * 128:(b + 1) * 128]
                           for t in range(4)])
            if m == 0:
                self.w(e, "casts", "cast", 3 * NC - 1)
                self.w(e, "dve1", "reps")
            for ts_ in range(4):
                if ts_ >= 2:
                    self.w(e, "dve1", "vevac", m, ts_ - 2)
                elif m > 0:
                    self.w(e, "dve1", "fstt", m - 1, ts_)
                for c in range(NC):
                    for h in range(NH):
                        last = (c == NC - 1 and h == NH - 1)
                        mm(ps['b' + str(ts_ % 2)][:, h * 512:(h + 1) * 512],
                           sb['xnf'][:, p, c, ts_ * 128:(ts_ + 1) * 128],
                           sb['wv'][:, c, h * 512:(h + 1) * 512],
                           c == 0, c == NC - 1,
                           "pe1" if last else None,
                           key=("vmmt", m, ts_) if last else None)
                if ts_ == 0:
                    self.w(e, "dve1", "xq", m)
                tpose(("tq", m, ts_),
                      [sb['xqt'][:, t, (2 * ts_ + cc) * 128:
                                 (2 * ts_ + cc + 1) * 128]
                       for cc in range(2) for t in range(2)])
            self.mark("pe1", "vmm", m)
            if m == 0:
                self.w(e, "sfwt", "fwt", NC - 1)
            self.w(e, "dve1", "tqev", m, 3)
            for ts_ in range(2):
                self.w(e, "dve1", "vevac", m, 2 + ts_)
                for c in range(NC):
                    for h in range(NH):
                        last = (c == NC - 1 and h == NH - 1)
                        mm(ps['b' + str(ts_ % 2)][:, h * 512:(h + 1) * 512],
                           sb['xqf'][:, p, c, ts_ * 128:(ts_ + 1) * 128],
                           sb['fwt'][:, c, h * 512:(h + 1) * 512],
                           c == 0, c == NC - 1,
                           "pe1" if last else None,
                           key=("fmmt", m, ts_) if last else None)
            self.mark("pe1", "fmm", m)

        def av_mms(i, gg, m):
            p2 = i % 2
            self.w(e, "svb" + str(gg % 2), "g", gg)
            self.wv(e, "dveu", 2 * i)
            for su in range(2):
                for ks in range(KSN):
                    if gg == m:
                        self.w(e, "dmask", "dm", i, ks)
                    else:
                        self.w(e, "aexp", "exp", i, ks)
                    for h in range(NH):
                        last = (ks == KSN - 1 and h == NH - 1)
                        mm(ps['b' + str(su)][:, h * 512:(h + 1) * 512],
                           sb['et'][:, p2, ks, su * 128:(su + 1) * 128],
                           sb['vb'][:, gg % 2, ks, h * 512:(h + 1) * 512],
                           ks == 0, ks == KSN - 1,
                           "peav" if last else None,
                           key=("avs", i, su) if last else None)
            self.mark("peav", "jdone", i)

        for i, (gg, m) in enumerate(self.jobs):
            self.w(e, "skv" + str(gg % 2), "gk", gg)
            self.w(e, "sqj" + str(i % 2), "q", i)
            for ks in range(KSN):
                if ks >= 2:
                    self.w(e, "aexp", "exp", i, ks - 2)
                elif i > 0:
                    self.w(e, "aexp", "exp", i - 1, KSN - 2 + ks)
                for c in range(NC):
                    last = c == NC - 1
                    mm(ps['a' + str(ks % 2)][:, :],
                       sb['kb'][:, gg % 2, ks // 4, c,
                                (ks % 4) * 128:(ks % 4 + 1) * 128],
                       sb['qb'][:, i % 2, c, :], c == 0, last,
                       "pesc" if last else None,
                       key=("sc", i, ks) if last else None)
            self.mark("pesc", "scj", i)
            av_mms(i, gg, m)

        self.w(e, "dve0", "zred")
        mm(ps['a0'][0:1, 0:1], sb['alph'][:, 0:1], sb['ones'][:, 0:1], True, True,
           "pe0", key=("zmm",))
        self.wv(e, "zbar", 1)
        self.w(e, "dve0", "ztot")
        mm(ps['a0'][:, 2:3], sb['onesrow'][0:1, :], sb['zme'][0:1, 1:2], True, True,
           "pe0", key=("zrep",))


def build_program(cfg):
    b = Builder(cfg)
    nc = b.build()
    return nc, b




def kernel(L0_attn_x, L0_attn_Wq, L0_attn_bq, L0_attn_Wk, L0_attn_bk,
           L0_attn_Wv, L0_attn_bv, L0_ffn_weight, L0_ffn_bias, _cfg=None):
    cfg = _cfg or Cfg(T=16384, D=1024)
    x = np.ascontiguousarray(np.asarray(L0_attn_x, dtype=np.float32))
    sched = schedule(cfg)
    nc, _b = build_program(cfg)

    in_maps = []
    for c in range(8):
        bat = c // 4
        per_m = sched[c]
        rows = []
        for (t_av, t_so) in per_m:
            rows.append(x[bat, t_av * 256:(t_av + 1) * 256, :])
            rows.append(x[bat, t_so * 256:(t_so + 1) * 256, :])
        x_shard = np.ascontiguousarray(np.concatenate(rows, axis=0))
        tbl = np.zeros((1, max(2 * cfg.NM, 16)), dtype=np.int32)
        for m, (t_av, t_so) in enumerate(per_m):
            tbl[0, 2 * m] = t_av
            tbl[0, 2 * m + 1] = t_so
        in_maps.append({
            "x_in": x_shard,
            "wq_in": np.ascontiguousarray(np.asarray(L0_attn_Wq, dtype=np.float32)),
            "wk_in": np.ascontiguousarray(np.asarray(L0_attn_Wk, dtype=np.float32)),
            "wv_in": np.ascontiguousarray(np.asarray(L0_attn_Wv, dtype=np.float32)),
            "fw_in": np.ascontiguousarray(np.asarray(L0_ffn_weight, dtype=np.float32)),
            "bq_in": np.asarray(L0_attn_bq, dtype=np.float32).reshape(1, -1),
            "bk_in": np.asarray(L0_attn_bk, dtype=np.float32).reshape(1, -1),
            "bv_in": np.asarray(L0_attn_bv, dtype=np.float32).reshape(1, -1),
            "fb_in": np.asarray(L0_ffn_bias, dtype=np.float32).reshape(1, -1),
            "msk_in": np.ascontiguousarray(host_masks(cfg, sched, c)[0]),
            "id_in": np.eye(128, dtype=ml_dtypes.bfloat16),
            "tbl_in": tbl,
        })

    res = run_bass_kernel_spmd(nc, in_maps, core_ids=list(range(8)))

    out = np.zeros((cfg.B, cfg.T, cfg.D), dtype=np.float32)
    for c in range(8):
        bat = c // 4
        o = np.asarray(res.results[c]["out_t"])
        for m, (t_av, _t_so) in enumerate(sched[c]):
            out[bat, t_av * 256:(t_av + 1) * 256, :] = o[m * 256:(m + 1) * 256, :]
    return out
